# version 08
# baseline (speedup 1.0000x reference)
import numpy as np
import ml_dtypes
import concourse.bass as bass
import concourse.mybir as mybir
from concourse.alu_op_type import AluOpType as OP
from concourse.bass_utils import run_bass_kernel_spmd

F32 = mybir.dt.float32
BF16 = mybir.dt.bfloat16
AF = mybir.ActivationFunctionType
AX = mybir.AxisListType
BF = ml_dtypes.bfloat16

D = 2048
S = 2048
B = 4
TOK = 1024
H = 16
HD = 128
MYH = 8
F = 8192
E, R = 8, 16
ER = E * R
KC = D // 128
NEG = -1e30
EPS_LN = 1e-5
EPS_L2 = 1e-6
SCALE_ATT = float(1.0 / np.sqrt(HD))
CUT = 99
DEV_TAPS = False
RACE_DETECT = True

ENGS = ("pe", "act", "dve", "po", "sp")
BLOCK_ATTR = {"pe": "tensor", "act": "scalar", "dve": "vector", "po": "gpsimd", "sp": "sync"}
SEMS = ("pe", "act", "dve", "po", "cst", "xs", "wmisc",
        "wA0", "wA1", "wB0", "wB1", "wC0", "wC1",
        "xg0", "xg1", "xg2", "xg3", "xb0", "xb1", "xb2", "xb3",
        "cc0", "cc1", "cc2", "cc3", "cs", "qv", "wo0", "wo1", "ccx")


class Prog:

    def __init__(self):
        self.steps = {e: [] for e in ENGS}
        self.cnt = dict.fromkeys(SEMS, 0)
        self.lastw = {e: dict.fromkeys(SEMS, 0) for e in ENGS}
        self.pend = {e: [] for e in ENGS}

    def want(self, eng, *mss):
        for ms in mss:
            if ms is not None:
                self.pend[eng].append(ms)

    def step(self, eng, fn, sem=None, amt=1):
        if eng in ("dve", "act") and self.cnt[eng] > 0:
            self.pend[eng].append((eng, self.cnt[eng]))
        waits = []
        for (s, v) in self.pend[eng]:
            if v > self.lastw[eng][s]:
                waits.append((s, v))
                self.lastw[eng][s] = v
        self.pend[eng] = []
        ms = None
        if sem is not None:
            self.cnt[sem] += amt
            ms = (sem, self.cnt[sem])
        self.steps[eng].append((fn, (sem, amt) if sem else None, tuple(waits)))
        return ms

    def barrier_ms(self):
        return [(s, self.cnt[s]) for s in ("pe", "act", "dve", "po")]

    def full_barrier(self):
        mss = [(s, self.cnt[s]) for s in SEMS if self.cnt[s] > 0]
        for e in ENGS:
            self.want(e, *mss)

    def realize(self, block, sems):
        for eng in ENGS:
            steps = self.steps[eng]

            def make(steps):
                def body(h):
                    for fn, inc, waits in steps:
                        for (s, v) in waits:
                            h.wait_ge(sems[s], v)
                        r = fn(h)
                        if inc is not None:
                            r.then_inc(sems[inc[0]], inc[1])
                return body
            getattr(block, BLOCK_ATTR[eng])(make(steps))


def host_consts():
    c = {}
    inv = 1.0 / 10000.0 ** (np.arange(0, HD, 2, dtype=np.float64) / HD)
    ang = np.outer(inv, np.arange(S, dtype=np.float64))
    cosf = np.repeat(np.cos(ang), 2, axis=0).astype(np.float32)
    sinf = np.repeat(np.sin(ang), 2, axis=0).astype(np.float32)
    sgn = np.tile(np.array([-1.0, 1.0], np.float32), HD // 2)[:, None]
    c["cosf"] = cosf.astype(BF)
    c["sinf"] = (sinf * sgn).astype(BF)
    c["ident"] = np.eye(128, dtype=BF)
    c["ident_f"] = np.eye(128, dtype=np.float32)
    keep = np.zeros((4, 128, 512), np.float32)
    for j0 in range(4):
        keep[j0, :, (j0 + 1) * 128:] = 1.0
        keep[j0, :, j0 * 128:(j0 + 1) * 128] = np.triu(np.ones((128, 128), np.float32), 0)
    c["keep"] = np.concatenate([keep[j] for j in range(4)], axis=1).astype(BF)
    ml = -np.tril(np.ones((128, 128), np.float32), -1)
    mu = -np.triu(np.ones((128, 128), np.float32), 1)
    c["maskl4"] = np.tile(ml, (1, 4)).astype(BF)
    c["masku4"] = np.tile(mu, (1, 4)).astype(BF)
    c["E16"] = np.repeat(np.eye(E, dtype=BF), R, axis=1)
    c["onehot8"] = np.eye(8, dtype=BF)
    oh32 = np.zeros((128, 8), BF)
    oh32[32:36, 0:4] = np.eye(4, dtype=BF)
    c["oh32"] = oh32
    c["ones_row"] = np.ones((1, 128), np.float32)
    c["ones_row_bf"] = np.ones((1, 128), BF)
    c["ones_col_bf"] = np.ones((128, 1), BF)
    return c


def host_prep(inputs):
    x = np.asarray(inputs["L0_attn_a_x"], np.float32)
    perm = np.zeros(HD, np.int64)
    perm[0::2] = np.arange(64)
    perm[1::2] = np.arange(64) + 64

    aWq = np.asarray(inputs["L0_attn_a_Wq"], np.float32)
    aWk = np.asarray(inputs["L0_attn_a_Wk"], np.float32)
    aWv = np.asarray(inputs["L0_attn_a_Wv"], np.float32)
    aWo = np.asarray(inputs["L0_attn_a_Wo"], np.float32)
    bWq = np.asarray(inputs["L0_attn_b_Wq"], np.float32).reshape(D, H, HD)[:, :, perm].reshape(D, D)
    bWk = np.asarray(inputs["L0_attn_b_Wk"], np.float32).reshape(D, H, HD)[:, :, perm].reshape(D, D)
    bWv = np.asarray(inputs["L0_attn_b_Wv"], np.float32)
    bWo = np.asarray(inputs["L0_attn_b_Wo"], np.float32)
    bWb = np.asarray(inputs["L0_attn_b_Wb"], np.float32)
    Wr = np.asarray(inputs["L0_ffn_Wr"], np.float32)
    br = np.asarray(inputs["L0_ffn_br"], np.float32)
    W1 = np.asarray(inputs["L0_ffn_W1"], np.float32)
    b1 = np.asarray(inputs["L0_ffn_b1"], np.float32)
    W2 = np.asarray(inputs["L0_ffn_W2"], np.float32)
    b2 = np.asarray(inputs["L0_ffn_b2"], np.float32)
    A1s = np.ascontiguousarray(np.asarray(inputs["L0_ffn_A1"], np.float32).transpose(2, 0, 1).reshape(D, ER))
    B1s = np.ascontiguousarray(np.asarray(inputs["L0_ffn_B1"], np.float32).transpose(0, 2, 1).reshape(ER, F))
    A2s = np.ascontiguousarray(np.asarray(inputs["L0_ffn_A2"], np.float32).transpose(2, 0, 1).reshape(F, ER))
    B2s = np.ascontiguousarray(np.asarray(inputs["L0_ffn_B2"], np.float32).transpose(0, 2, 1).reshape(ER, D))

    consts = host_consts()
    in_maps = []
    for core in range(8):
        b, half = core // 2, core % 2
        hs = half * MYH * HD
        m = {
            "xT": np.ascontiguousarray(x[b].T[:, half * TOK:(half + 1) * TOK]),
            "aWq": np.ascontiguousarray(aWq[:, hs:hs + MYH * HD]),
            "aWk": np.ascontiguousarray(aWk[:, hs:hs + MYH * HD]),
            "aWv": np.ascontiguousarray(aWv[:, hs:hs + MYH * HD]),
            "aWo": aWo,
            "bWq": np.ascontiguousarray(bWq[:, hs:hs + MYH * HD]),
            "bWk": np.ascontiguousarray(bWk[:, hs:hs + MYH * HD]),
            "bWv": np.ascontiguousarray(bWv[:, hs:hs + MYH * HD]),
            "bWo": bWo,
            "bWb": np.ascontiguousarray(bWb[:, half * MYH:(half + 1) * MYH]),
            "Wr": Wr, "W1": W1, "W2": W2,
            "A1s": A1s, "B1s": B1s, "A2s": A2s, "B2s": B2s,
            "br_row": br.reshape(1, E).astype(BF),
            "b1_col": np.ascontiguousarray(b1.reshape(F // 128, 128).T),
            "b2_row": b2.reshape(1, D).astype(BF),
            "par": np.full((128, 1), float(half), np.float32),
            "npar": np.full((128, 1), float(1 - half), np.float32),
        }
        m.update(consts)
        in_maps.append(m)
    return in_maps


class Ctx:
    pass


class BitView:

    def __init__(self, t):
        self.t = t

    def __getitem__(self, idx):
        p, s = idx
        return self.t[p, s.start // 2: s.stop // 2].bitcast(BF16)


def build_program(sublayers=("attn_a", "ffn", "attn_b", "ffn")):
    import contextlib
    nc = bass.Bass("TRN2", target_bir_lowering=False, detect_race_conditions=RACE_DETECT)
    P = Prog()
    c = Ctx()
    c.nc, c.P = nc, P
    c.psum_guard = []
    c.stage_guard = {}

    dr = {}
    def din(name, shape, dt=F32):
        dr[name] = nc.dram_tensor(name, list(shape), dt, kind="ExternalInput").ap()
    din("xT", (D, TOK))
    for w in ("aWq", "aWk", "aWv", "bWq", "bWk", "bWv"):
        din(w, (D, MYH * HD))
    din("aWo", (D, D)); din("bWo", (D, D)); din("bWb", (D, MYH))
    din("Wr", (D, E)); din("W1", (D, F)); din("W2", (F, D))
    din("A1s", (D, ER)); din("B1s", (ER, F)); din("A2s", (F, ER)); din("B2s", (ER, D))
    din("br_row", (1, E), BF16); din("b1_col", (128, F // 128)); din("b2_row", (1, D), BF16)
    din("par", (128, 1)); din("npar", (128, 1))
    din("cosf", (128, S), BF16); din("sinf", (128, S), BF16)
    din("ident", (128, 128), BF16); din("ident_f", (128, 128))
    din("keep", (128, 2048), BF16)
    din("maskl4", (128, 512), BF16); din("masku4", (128, 512), BF16)
    din("E16", (E, 128), BF16); din("onehot8", (E, E), BF16); din("oh32", (128, E), BF16); din("ones_row", (1, 128)); din("ones_row_bf", (1, 128), BF16)
    din("ones_col_bf", (128, 1), BF16)
    c.dr = dr
    out_ext = nc.dram_tensor("out", [D, TOK], F32, kind="ExternalOutput").ap()
    if DEV_TAPS:
        c.tapA = nc.dram_tensor("tapA", [128, 8192], F32, kind="ExternalOutput").ap()
        c.tapB = nc.dram_tensor("tapB", [128, 8192], F32, kind="ExternalOutput").ap()
        c.tapC = nc.dram_tensor("tapC", [128, 8192], F32, kind="ExternalOutput").ap()
        c.tapD = nc.dram_tensor("tapD", [128, 8192], F32, kind="ExternalOutput").ap()

    c.xspill = nc.dram_tensor("xspill", [128, 16 * TOK], F32)
    c.xn_g = [nc.dram_tensor(f"xn_g{g}", [4 * 128, TOK], BF16) for g in range(4)]
    c.xn_go = [nc.dram_tensor(f"xn_go{g}", [8 * 128, TOK], BF16) for g in range(4)]
    c.ctx_st = nc.dram_tensor("ctx_st", [MYH * 128, S], BF16)
    c.ctx_out = nc.dram_tensor("ctx_out", [H * 128, S], BF16)

    es = contextlib.ExitStack()
    with es:
        sb = lambda name, shape, dt: es.enter_context(nc.sbuf_tensor("s_" + name, list(shape), dt))
        c.ident = sb("ident", (128, 128), BF16)
        c.ident_f = sb("identf", (128, 128), F32)
        c.maskl4 = sb("maskl4", (128, 512), BF16)
        c.masku4 = sb("masku4", (128, 512), BF16)
        c.onehot8 = sb("onehot8", (E, E), BF16)
        c.oh32 = sb("oh32", (128, E), BF16)
        c.neg8 = sb("neg8", (128, 1), F32)
        c.E16 = sb("E16c", (E, 128), BF16)
        c.ones_row = sb("ones_row", (1, 128), F32)
        c.ones_row_bf = sb("ones_row_b", (1, 128), BF16)
        c.ones_col_bf = sb("ones_col_b", (128, 1), BF16)
        c.par = sb("par", (128, 1), F32)
        c.npar = sb("npar", (128, 1), F32)
        c.b1_col = sb("b1_colS", (128, F // 128), F32)
        c.br_row = sb("br_rowS", (1, E), BF16)
        c.epsln = sb("epsln", (1, 1), F32)
        c.epsl2 = sb("epsl2", (1, 1), F32)
        c.r_row = sb("r_row", (1, 512), F32)
        c.nmr_row = sb("nmr_row", (1, 512), F32)
        c.rowscr = sb("rowscr", (1, 512), F32)
        c.rowscr2 = sb("rowscr2", (1, 512), F32)
        XU = sb("XU", (128, 16 * TOK), F32)
        c.PSF = es.enter_context(nc.psum_tensor("PSF", [128, 2048], F32))
        c.PSG = es.enter_context(nc.psum_tensor("PSG", [128, 2048], F32))
        c.XR = XU
        c.XNF = BitView(XU)
        c.wstA = [sb(f"wstA{i}", (128, 2048), BF16) for i in range(2)]
        c.wstB = [sb(f"wstB{i}", (128, 2048), BF16) for i in range(2)]
        c.wstC = [sb(f"wstC{i}", (128, 256), BF16) for i in range(2)]

        emit_p0(c)
        c.xr_sources = [c.xspill_ms]

        i = 0
        pending_wo = None
        first = True
        while i <= len(sublayers):
            emit_restore_x(c, first=first)
            if pending_wo is not None:
                emit_attn_post(c, pending_wo)
                pending_wo = None
            while i < len(sublayers) and sublayers[i] == "ffn":
                emit_ffn(c, i)
                i += 1
            if i < len(sublayers):
                kind = sublayers[i]
                emit_attn_pre(c, i, kind)
            else:
                emit_store_out(c, out_ext)
                break
            emit_attn_body(c, i, sublayers[i])
            pending_wo = sublayers[i]
            i += 1
            first = False

        sems = {}
        with contextlib.ExitStack() as es2:
            for s in SEMS:
                sems[s] = es2.enter_context(nc.semaphore(s))
            with nc.Block() as block:
                P.realize(block, sems)
    return nc, c


def emit_p0(c):
    P, dr = c.P, c.dr
    loads = [
        (c.ident, "ident"), (c.ident_f, "ident_f"),
        (c.maskl4, "maskl4"), (c.masku4, "masku4"), (c.onehot8, "onehot8"), (c.oh32, "oh32"),
        (c.E16, "E16"), (c.ones_row, "ones_row"), (c.ones_row_bf, "ones_row_bf"),
        (c.ones_col_bf, "ones_col_bf"), (c.par, "par"), (c.npar, "npar"),
        (c.b1_col, "b1_col"), (c.br_row, "br_row"),
    ]
    for t, name in loads:
        P.step("sp", lambda h, t=t, name=name: h.dma_start(out=t[:, :], in_=dr[name]), sem="cst", amt=16)
    P.step("po", lambda h: h.memset(c.epsln[0:1, 0:1], EPS_LN), sem="po")
    P.step("po", lambda h: h.memset(c.epsl2[0:1, 0:1], EPS_L2), sem="po")
    P.step("po", lambda h: h.memset(c.neg8[:, 0:1], -8.0), sem="po")
    def fn(h):
        return h.dma_start(out=c.xspill[:, :].rearrange("p (c t) -> p c t", c=KC), in_=dr["xT"].rearrange("(c p) t -> p c t", p=128))
    P.step("sp", fn, sem="xs", amt=16)
    c.consts_ms = ("cst", P.cnt["cst"])
    c.xspill_ms = ("xs", P.cnt["xs"])


def emit_restore_x(c, first=False):
    P, dr = c.P, c.dr
    P.want("sp", *c.xr_sources)
    c.xr_chunk = {}
    for kc in range(KC):
        if first:
            P.step("sp", lambda h, kc=kc: h.dma_start(
                out=c.XR[:, kc * TOK:(kc + 1) * TOK],
                in_=dr["xT"][kc * 128:(kc + 1) * 128, :]), sem="xs", amt=16)
        else:
            P.step("sp", lambda h, kc=kc: h.dma_start(
                out=c.XR[:, kc * TOK:(kc + 1) * TOK],
                in_=c.xspill[:, kc * TOK:(kc + 1) * TOK]), sem="xs", amt=16)
        c.xr_chunk[kc] = [("xs", P.cnt["xs"])]
    c.xr_ready = [("xs", P.cnt["xs"])]
    c.xr_dirty = False


def emit_spill_x(c):
    P = c.P
    for kc in range(KC):
        P.want("sp", *c.xr_chunk[kc])
        P.step("sp", lambda h, kc=kc: h.dma_start(
            out=c.xspill[:, kc * TOK:(kc + 1) * TOK],
            in_=c.XR[:, kc * TOK:(kc + 1) * TOK]), sem="xs", amt=16)
    c.xr_sources = [("xs", P.cnt["xs"])]
    c.xnf_region_guard = [("xs", P.cnt["xs"])]


def emit_store_out(c, out_ext):
    P = c.P
    P.want("sp", *c.xr_ready)
    P.step("sp", lambda h: h.dma_start(out=out_ext.rearrange("(c p) t -> p c t", p=128), in_=c.XR[:, :].rearrange("p (c t) -> p c t", c=KC)),
           sem="xs", amt=16)
    fin = ("xs", P.cnt["xs"])
    P.want("po", fin)
    P.step("po", lambda h: h.memset(c.ones_row_bf[0:1, 0:1], 1.0))



def emit_ln(c, psA, psB, psC, psD, xn_sink, sq_scr, xbf_scr, t_scr):
    P = c.P
    hold_pe = []
    stat_ms = {}
    for half in range(2):
        co = half * 512
        pm2 = None
        pm_hist = {}
        for kc in range(KC):
            xr = c.XR[:, kc * TOK + co: kc * TOK + co + 512]
            P.want("dve", *c.xr_chunk[kc])
            if kc >= 2:
                P.want("dve", pm_hist[kc - 2][0], pm_hist[kc - 2][1])
            m1 = P.step("dve", lambda h, xr=xr, d=xbf_scr[kc % 2]: h.tensor_copy(d[:, 0:512], xr), sem="dve")
            m2 = P.step("dve", lambda h, s=xbf_scr[kc % 2], d=sq_scr[kc % 2]: h.tensor_tensor(d[:, 0:512], s[:, 0:512], s[:, 0:512], OP.mult), sem="dve")
            P.want("pe", m1, m2)
            if kc == 0:
                P.want("pe", *hold_pe)
            def f1(h, s=xbf_scr[kc % 2], st=(kc == 0), sp=(kc == KC - 1)):
                return h.matmul(psA[0:1, 0:512], c.ones_col_bf[:, :], s[:, 0:512], start=st, stop=sp)
            def f2(h, s=sq_scr[kc % 2], st=(kc == 0), sp=(kc == KC - 1)):
                return h.matmul(psB[0:1, 0:512], c.ones_col_bf[:, :], s[:, 0:512], start=st, stop=sp)
            pm1 = P.step("pe", f1, sem="pe")
            pm2 = P.step("pe", f2, sem="pe")
            pm_hist[kc] = (pm1, pm2)
        rs = c.rowscr[0:1, 0:512]
        rs2 = c.rowscr2[0:1, 0:512]
        rr = c.r_row[0:1, 0:512]
        nm = c.nmr_row[0:1, 0:512]
        P.want("act", pm2)
        a1 = P.step("act", lambda h, rs=rs: h.activation(rs, psA[0:1, 0:512], AF.Square, scale=1.0 / D), sem="act")
        P.want("dve", a1)
        d1 = P.step("dve", lambda h, rs=rs, rs2=rs2: h.scalar_tensor_tensor(rs2, psB[0:1, 0:512], 1.0 / D, rs, OP.mult, OP.subtract), sem="dve")
        P.want("act", d1)
        P.step("act", lambda h, rs2=rs2: h.activation(rs2, rs2, AF.Ln, bias=c.epsln[0:1, 0:1]), sem="act")
        a2 = P.step("act", lambda h, rr=rr, rs2=rs2: h.activation(rr, rs2, AF.Exp, scale=-0.5), sem="act")
        d2 = a2
        if half == 1:
            P.want("act", *stat_ms[0])
        a3 = P.step("act", lambda h, rs=rs: h.activation(rs, psA[0:1, 0:512], AF.Copy, scale=1.0 / D), sem="act")
        P.want("dve", a3, d2)
        P.step("dve", lambda h, rs=rs, rr=rr, nm=nm: h.tensor_tensor(nm, rs, rr, OP.mult), sem="dve")
        d4 = P.step("dve", lambda h, nm=nm: h.tensor_scalar(nm, nm, -1.0, None, OP.mult, OP.bypass), sem="dve")
        hold_pe = [a3, d1]
        P.want("pe", d2, d4)
        pcs = (psC, psD) if half == 0 else (psA, psB)
        if half == 1:
            P.want("pe", *hold_pe)
        b1 = P.step("pe", lambda h, pcs=pcs, rr=rr: h.matmul(pcs[0][:, 0:512], c.ones_row[:, :], rr, start=True, stop=True), sem="pe")
        b2 = P.step("pe", lambda h, pcs=pcs, nm=nm: h.matmul(pcs[1][:, 0:512], c.ones_row[:, :], nm, start=True, stop=True), sem="pe")
        stat_ms[half] = (b1, b2)
    b1, b2 = stat_ms[0]
    b3, b4 = stat_ms[1]
    bc = {0: (psC, psD), 1: (psA, psB)}
    sink_last = None
    for kc in range(KC):
        for half in range(2):
            co = half * 512
            psCh, psDh = bc[half]
            xr = c.XR[:, kc * TOK + co: kc * TOK + co + 512]
            P.want("dve", b1, b2, b3, b4)
            m1 = P.step("dve", lambda h, xr=xr, d=t_scr[kc % 2], psCh=psCh: h.tensor_tensor(d[:, 0:512], xr, psCh[:, 0:512], OP.mult), sem="dve")
            sink_last = xn_sink(kc, half, m1, t_scr[kc % 2], psDh)
    P.want("pe", sink_last)
    return sink_last



def emit_ffn(c, si):
    import contextlib
    P, nc, dr = c.P, c.nc, c.dr
    tag = f"f{si}_"
    with contextlib.ExitStack() as es:
        sb = lambda name, shape, dt: es.enter_context(nc.sbuf_tensor(tag + name, list(shape), dt))
        XNB = sb("XNB", (128, 16 * TOK), BF16)
        MIX = sb("MIX", (128, 16 * TOK), BF16)
        LG = sb("LG", (128, 8 * E), F32)
        M0TM = sb("M0TM", (128, 8 * E), F32)
        M1TM = sb("M1TM", (128, 8 * E), F32)
        SCR8 = sb("SCR8", (128, E), F32)
        JNK8 = sb("JNK8", (128, E), F32)
        Z0 = sb("Z0", (128, 8), F32)
        NZ0 = sb("NZ0", (128, 8), F32)
        Z1 = sb("Z1", (128, 8), F32)
        DEN = sb("DEN", (128, 8), F32)
        W0c = sb("W0c", (128, 8), F32)
        W1c = sb("W1c", (128, 8), F32)
        M0S = sb("M0S", (E, TOK), BF16)
        M1S = sb("M1S", (E, TOK), BF16)
        M0FB = sb("M0F", (128, TOK), BF16)
        M1FB = sb("M1F", (128, TOK), BF16)
        ROWSBF = sb("ROWSBF", (1, 3 * TOK), BF16)
        WB = sb("WB", (128, 2 * TOK), BF16)
        T1M = sb("T1M", (128, 2 * TOK), BF16)
        T2SB = sb("T2SB", (128, 2 * TOK), BF16)
        HSB = sb("HSB", (128, 1024), BF16)
        ASB = sb("ASB", (128, 1024), BF16)
        CASB = sb("CASB", (128, 1024), BF16)
        A1ST = sb("A1ST", (128, 16 * 128), BF16)
        B2ST = sb("B2ST", (128, D), BF16)
        XBF = [sb(f"XBF{i}", (128, 512), BF16) for i in range(2)]
        SQS = [sb(f"SQS{i}", (128, 512), BF16) for i in range(2)]
        _ts = sb("TS0", (128, 512), F32)
        TS = [_ts, _ts]

        PX = [c.PSF[:, 0:512], c.PSF[:, 512:1024]]
        PY = [c.PSF[:, 1024:1536], c.PSF[:, 1536:2048]]
        PT2 = c.PSG[:, :]

        P.full_barrier()
        c.psum_guard = []

        def xn_sink(kc, half, t_ms, t_src, psDh):
            dst = XNB[:, kc * TOK + half * 512: kc * TOK + half * 512 + 512]
            return P.step("dve", lambda h, d=dst, s=t_src, psDh=psDh: h.tensor_tensor(d, s[:, 0:512], psDh[:, 0:512], OP.add), sem="dve")
        if CUT < 1:
            c.psum_guard = P.barrier_ms()
            P.full_barrier()
            return
        emit_ln(c, PX[0], PX[1], PY[0], PY[1], xn_sink, SQS, XBF, TS)
        xn_done = P.barrier_ms()
        ln_psum_readers = list(xn_done)

        P.want("po", c.consts_ms, *c.stage_guard.get("wstC0") or [])
        P.step("po", lambda h: h.dma_start(out=c.wstC[0][:, 0:KC * E].rearrange("p (k e) -> p k e", k=KC), in_=dr["Wr"].rearrange("(k p) e -> p k e", p=128)), sem="wC0", amt=16)
        wr_ms = ("wC0", P.cnt["wC0"])
        P.step("po", lambda h: h.dma_start(out=A1ST[:, :].rearrange("p (k e) -> p k e", k=KC), in_=dr["A1s"].rearrange("(k p) e -> p k e", p=128)), sem="wmisc", amt=16)
        P.step("po", lambda h: h.dma_start(out=B2ST[:, :], in_=dr["B2s"]), sem="wmisc", amt=16)
        a1_ms = b2_ms = ("wmisc", P.cnt["wmisc"])

        lg_ms = []
        for t in range(8):
            P.want("pe", wr_ms, *(xn_done if t == 0 else []), *(ln_psum_readers if t == 0 else []))
            n = KC + 1
            for kc in range(KC):
                def fmm(h, kc=kc, t=t, st=(kc == 0)):
                    return h.matmul(PX[0][:, 0:E], XNB[:, kc * TOK + t * 128: kc * TOK + t * 128 + 128],
                                    c.wstC[0][:, kc * E: kc * E + E], start=st, stop=False)
                P.step("pe", fmm)
            pm = P.step("pe", lambda h: h.matmul(PX[0][:, 0:E], c.ones_row_bf[:, :], c.br_row[:, :], start=False, stop=True), sem="pe")
            P.want("act", pm)
            am = P.step("act", lambda h, t=t: h.activation(LG[:, t * E:(t + 1) * E], PX[0][:, 0:E], AF.Copy), sem="act")
            P.want("pe", am)
            lg_ms.append(am)
        if CUT < 2:
            c.psum_guard = P.barrier_ms()
            P.full_barrier()
            return
        c.stage_guard["wstC0"] = [("pe", P.cnt["pe"])]
        dlast = None
        for t in range(8):
            sl = LG[:, t * E:(t + 1) * E]
            P.want("dve", lg_ms[t])
            P.step("dve", lambda h, sl=sl, t=t: h.reduce_max(Z0[:, t:t + 1], sl, axis=AX.X), sem="dve")
            P.step("dve", lambda h, t=t: h.tensor_scalar(NZ0[:, t:t + 1], Z0[:, t:t + 1], -1.0, None, OP.mult, OP.bypass), sem="dve")
            P.step("dve", lambda h, sl=sl, t=t: h.tensor_scalar(M0TM[:, t * E:(t + 1) * E], sl, Z0[:, t:t + 1], None, OP.is_equal, OP.bypass), sem="dve")
            P.step("dve", lambda h, sl=sl, t=t: h.scalar_tensor_tensor(SCR8[:, 0:E], M0TM[:, t * E:(t + 1) * E], NEG, sl, OP.mult, OP.add), sem="dve")
            P.step("dve", lambda h, t=t: h.reduce_max(Z1[:, t:t + 1], SCR8[:, 0:E], axis=AX.X), sem="dve")
            P.step("dve", lambda h, t=t: h.tensor_scalar(M1TM[:, t * E:(t + 1) * E], SCR8[:, 0:E], Z1[:, t:t + 1], None, OP.is_equal, OP.bypass), sem="dve")
            dlast = P.step("dve", lambda h, t=t: h.tensor_tensor(W1c[:, t:t + 1], Z1[:, t:t + 1], NZ0[:, t:t + 1], OP.add), sem="dve")
        a1 = None
        for t in range(8):
            sl = LG[:, t * E:(t + 1) * E]
            if t == 0:
                P.want("act", dlast)
            a1 = P.step("act", lambda h, sl=sl, t=t: h.activation(JNK8[:, 0:E], sl, AF.Exp, bias=NZ0[:, t:t + 1], accum_out=DEN[:, t:t + 1]), sem="act")
        a2 = P.step("act", lambda h: h.activation(W1c[:, 0:8], W1c[:, 0:8], AF.Exp), sem="act")
        P.want("dve", a1, a2)
        P.step("dve", lambda h: h.reciprocal(W0c[:, 0:8], DEN[:, 0:8]), sem="dve")
        d9 = P.step("dve", lambda h: h.tensor_tensor(W1c[:, 0:8], W1c[:, 0:8], W0c[:, 0:8], OP.mult), sem="dve")
        row_src = [d9] * 8
        if CUT < 2.7:
            c.psum_guard = P.barrier_ms()
            P.full_barrier()
            return
        last_row_act = None
        for name, col, off in (("w0", W0c, 0), ("w1", W1c, TOK)):
            P.want("pe", row_src[0], *( [last_row_act] if last_row_act else [] ))
            pm = None
            for t in range(8):
                pm = P.step("pe", lambda h, col=col, t=t: h.matmul(PT2[0:1, t * 128:(t + 1) * 128], col[:, t:t + 1], c.ident_f[:, :], start=True, stop=True), sem="pe" if t == 7 else None)
            P.want("act", pm)
            last_row_act = P.step("act", lambda h, off=off: h.activation(ROWSBF[0:1, off: off + TOK], PT2[0:1, 0:TOK], AF.Copy), sem="act")
            P.want("pe", last_row_act)
        if CUT < 3:
            c.psum_guard = P.barrier_ms()
            P.full_barrier()
            return
        P.want("dve", last_row_act)
        rbf = P.step("dve", lambda h: h.tensor_tensor(ROWSBF[0:1, 2 * TOK:3 * TOK], ROWSBF[0:1, 0:TOK], ROWSBF[0:1, TOK:2 * TOK], OP.add), sem="dve")
        P.want("pe", rbf)
        pm = None
        for j in range(4):
            pm = P.step("pe", lambda h, j=j: h.matmul(PT2[:, j * 512:(j + 1) * 512], c.ones_row_bf[:, :], ROWSBF[0:1, j * 512:(j + 1) * 512], start=True, stop=True), sem="pe" if j == 3 else None)
        P.want("act", pm)
        am = P.step("act", lambda h: h.activation(WB[:, :], PT2[:, :], AF.Copy), sem="act")
        P.want("pe", am)
        wb_ms = [am] * 4

        mask_ms = []
        for k, (MTM, MS, MF) in enumerate(((M0TM, M0S, M0FB), (M1TM, M1S, M1FB))):
            P.want("pe", row_src[0])
            pm = None
            for t in range(8):
                pm = P.step("pe", lambda h, MTM=MTM, t=t: h.transpose(PT2[0:E, t * 128:(t + 1) * 128], MTM[:, t * E:(t + 1) * E], c.ident_f[:, :]), sem="pe" if t == 7 else None)
            P.want("act", pm)
            am = P.step("act", lambda h, MS=MS: h.activation(MS[0:E, 0:TOK], PT2[0:E, 0:TOK], AF.Copy), sem="act")
            tr_ms = [am]
            for half in range(2):
                P.want("pe", *tr_ms)
                pm = P.step("pe", lambda h, MS=MS, half=half: h.matmul(PX[1][:, 0:512], c.E16[:, :], MS[0:E, half * 512: half * 512 + 512], start=True, stop=True), sem="pe")
                P.want("dve", pm)
                dm = P.step("dve", lambda h, MF=MF, half=half: h.tensor_scalar(MF[:, half * 512: half * 512 + 512], PX[1][:, 0:512], 2.0, None, OP.mult, OP.bypass), sem="dve")
                P.want("pe", dm)
                mask_ms.append(dm)
        if CUT < 4:
            c.psum_guard = P.barrier_ms()
            P.full_barrier()
            return
        pt2_free = [("act", P.cnt["act"]), ("pe", P.cnt["pe"])]

        t1_ms = []
        for half in range(2):
            P.want("pe", a1_ms, *pt2_free)
            for kc in range(KC):
                def fmm(h, kc=kc, half=half, st=(kc == 0), sp=(kc == KC - 1)):
                    return h.matmul(PY[0][:, 0:512], A1ST[:, kc * 128:(kc + 1) * 128],
                                    XNB[:, kc * TOK + half * 512: kc * TOK + half * 512 + 512], start=st, stop=sp)
                pm = P.step("pe", fmm, sem="pe" if kc == KC - 1 else None)
            for k, MF in enumerate((M0FB, M1FB)):
                P.want("dve", pm, mask_ms[2 * k + half])
                dm = P.step("dve", lambda h, k=k, half=half, MF=MF: h.tensor_tensor(
                    T1M[:, k * TOK + half * 512: k * TOK + half * 512 + 512], PY[0][:, 0:512],
                    MF[:, half * 512: half * 512 + 512], OP.mult), sem="dve")
                t1_ms.append(dm)
                P.want("pe", dm)

        if CUT < 5:
            c.psum_guard = P.barrier_ms()
            P.full_barrier()
            return
        B2R = A1ST[0:1, 0:D]
        P.want("sp", *(c.stage_guard.get("b2r") or []), ("pe", P.cnt["pe"]))
        P.step("sp", lambda h: h.dma_start(out=B2R, in_=dr["b2_row"]), sem="cst", amt=16)
        b2r_ms = ("cst", P.cnt["cst"])
        def stage_f(f):
            buf = f % 2
            P.want("po", *c.stage_guard.get(f"wstA{buf}") or [])
            m1 = P.step("po", lambda h, f=f, buf=buf: h.dma_start(
                out=c.wstA[buf][:, :].rearrange("p (k q) -> p k q", k=KC),
                in_=dr["W1"][:, f * 128:(f + 1) * 128].rearrange("(k p) q -> p k q", p=128)), sem=f"wA{buf}", amt=16)
            P.want("po", *c.stage_guard.get(f"wstC{buf}") or [])
            P.step("po", lambda h, f=f, buf=buf: h.dma_start(
                out=c.wstC[buf][:, 0:128], in_=dr["B1s"][:, f * 128:(f + 1) * 128]), sem=f"wC{buf}", amt=16)
            P.step("po", lambda h, f=f, buf=buf: h.dma_start(
                out=c.wstC[buf][:, 128:256], in_=dr["A2s"][f * 128:(f + 1) * 128, :]), sem=f"wC{buf}", amt=16)
            m3 = (f"wC{buf}", P.cnt[f"wC{buf}"])
            return [m1, m3]

        def stage_w2(sup, d, buf):
            P.want("po", *c.stage_guard.get(f"wstB{buf}") or [])
            return P.step("po", lambda h, sup=sup, d=d, buf=buf: h.dma_start(
                out=c.wstB[buf][:, :].rearrange("p (q dd) -> p q dd", q=KC),
                in_=dr["W2"][sup * 2048:(sup + 1) * 2048, d * 128:(d + 1) * 128].rearrange("(q p) dd -> p q dd", p=128)), sem=f"wB{buf}", amt=16)

        stage_ms = {0: stage_f(0)}
        ypend = {"fn": None}
        hsb2_rd = {0: [], 1: []}
        slot_t = {0: PX[0], 1: PX[1], 2: PY[0], 3: PY[1]}
        x_free = {0: [], 1: []}
        y_free = {0: [], 1: []}
        pd_free = {}
        hsb_rd = []
        asb_rd = {0: [], 1: []}
        casb_rd = {0: [], 1: []}
        w2_ms = {}
        mixw_guard = []
        for sup in range(4):
            first_mix = True
            for fi in range(16):
                f = sup * 16 + fi
                buf = f % 2
                for half in range(2):
                    X = PX[half]
                    P.want("pe", *stage_ms[f], *x_free[half])
                    for kc in range(KC):
                        def fmm(h, kc=kc, half=half, buf=buf, X=X, st=(kc == 0), sp=(kc == KC - 1)):
                            return h.matmul(X[:, 0:512], c.wstA[buf][:, kc * 128:(kc + 1) * 128],
                                            XNB[:, kc * TOK + half * 512: kc * TOK + half * 512 + 512], start=st, stop=sp)
                        pmx = P.step("pe", fmm, sem="pe" if kc == KC - 1 else None)
                    P.want("act", pmx, *hsb2_rd[half])
                    hm = P.step("act", lambda h, X=X, f=f, half=half: h.activation(
                        HSB[:, half * 512: half * 512 + 512], X[:, 0:512], AF.Identity,
                        bias=c.b1_col[:, f:f + 1]), sem="act")
                    x_free[half] = [hm]

                    def mk_yphase(f=f, fi=fi, half=half, buf=buf, hm=hm):
                        def emit_y():
                            hsb2_rd[half] = []
                            for k in range(2):
                                Y = PY[k]
                                P.want("pe", *y_free[k], t1_ms[2 * k + half], hm)
                                def fl1(h, k=k, half=half, buf=buf, Y=Y):
                                    return h.matmul(Y[:, 0:512], c.wstC[buf][:, 0:128],
                                                    T1M[:, k * TOK + half * 512: k * TOK + half * 512 + 512], start=True, stop=False)
                                P.step("pe", fl1)
                                def fcp(h, Y=Y, half=half):
                                    return h.matmul(Y[:, 0:512], c.ident[:, :], HSB[:, half * 512: half * 512 + 512], start=False, stop=True)
                                pmy = P.step("pe", fcp, sem="pe")
                                hsb2_rd[half].append(pmy)
                                P.want("act", pmy, *asb_rd[k])
                                gm = P.step("act", lambda h, k=k, Y=Y: h.activation(ASB[:, k * 512:(k + 1) * 512], Y[:, 0:512], AF.Gelu_apprx_tanh), sem="act")
                                y_free[k] = [gm]
                                P.want("dve", gm, wb_ms[2 * k + half], *casb_rd[k])
                                cm = P.step("dve", lambda h, k=k, half=half: h.tensor_tensor(
                                    CASB[:, k * 512:(k + 1) * 512], ASB[:, k * 512:(k + 1) * 512],
                                    WB[:, k * TOK + half * 512: k * TOK + half * 512 + 512], OP.mult), sem="dve")
                                asb_rd[k] = [cm]
                                P.want("pe", cm, stage_ms[f][1])
                                def ft2(h, k=k, half=half, buf=buf, st=(f == 0), sp=(f == 63)):
                                    sl = PT2[:, (2 * k + half) * 512: (2 * k + half) * 512 + 512]
                                    return h.matmul(sl, c.wstC[buf][:, 128:256], CASB[:, k * 512:(k + 1) * 512], start=st, stop=sp)
                                t2m_pe = P.step("pe", ft2, sem="pe")
                                casb_rd[k] = [t2m_pe]
                            c.stage_guard[f"wstC{buf}"] = [("pe", P.cnt["pe"])]
                            mmx = P.step("dve", lambda h, fi=fi, half=half: h.tensor_tensor(
                                MIX[:, fi * TOK + half * 512: fi * TOK + half * 512 + 512],
                                CASB[:, 0:512], CASB[:, 512:1024], OP.add), sem="dve")
                            casb_rd[0].append(mmx)
                            casb_rd[1].append(mmx)
                            return t2m_pe, mmx
                        return emit_y
                    if first_mix:
                        P.want("dve", *mixw_guard)
                        first_mix = False
                    if ypend["fn"] is not None:
                        t2m_pe, mm = ypend["fn"]()
                    ypend["fn"] = mk_yphase()
                    if half == 1 and f + 1 < 64:
                        stage_ms[f + 1] = stage_f(f + 1)
                c.stage_guard[f"wstA{buf}"] = [("pe", P.cnt["pe"])]
            if ypend["fn"] is not None:
                t2m_pe, mm = ypend["fn"]()
                ypend["fn"] = None
            mix_done = mm
            if sup == 3:
                t2sb_ms = []
                for k in range(2):
                    for half in range(2):
                        P.want("dve", t2m_pe, mask_ms[2 * k + half])
                        dm = P.step("dve", lambda h, k=k, half=half, MF=(M0FB if k == 0 else M1FB): h.tensor_tensor(
                            T2SB[:, k * TOK + half * 512: k * TOK + half * 512 + 512],
                            PT2[:, (2 * k + half) * 512:(2 * k + half) * 512 + 512],
                            MF[:, half * 512:half * 512 + 512], OP.mult), sem="dve")
                        t2sb_ms.append(dm)
            pd_free = {0: x_free[0], 1: x_free[1], 2: y_free[0], 3: y_free[1]}
            w2_ms[(sup, 0)] = stage_w2(sup, 0, 0)
            for d in range(16):
                if d + 1 < 16:
                    w2_ms[(sup, d + 1)] = stage_w2(sup, d + 1, (d + 1) % 2)
                wbuf = d % 2
                for half in range(2):
                    slot = (2 * d + half) % 4
                    Pd = slot_t[slot]
                    P.want("pe", w2_ms[(sup, d)], mix_done, *pd_free[slot])
                    last_plain = (sup != 3)
                    for fi in range(KC):
                        sp = last_plain and (fi == KC - 1)
                        def fw2(h, fi=fi, half=half, wbuf=wbuf, Pd=Pd, st=(fi == 0), sp=sp):
                            return h.matmul(Pd[:, 0:512], c.wstB[wbuf][:, fi * 128:(fi + 1) * 128],
                                            MIX[:, fi * TOK + half * 512: fi * TOK + half * 512 + 512],
                                            start=st, stop=sp)
                        pm = P.step("pe", fw2, sem="pe" if sp else None)
                    if sup == 3:
                        for k in range(2):
                            P.want("pe", t2sb_ms[2 * k + half], b2_ms, b2r_ms)
                            def fl2(h, k=k, half=half, d=d, Pd=Pd):
                                return h.matmul(Pd[:, 0:512], B2ST[:, d * 128:(d + 1) * 128],
                                                T2SB[:, k * TOK + half * 512: k * TOK + half * 512 + 512],
                                                start=False, stop=False)
                            P.step("pe", fl2)
                        def fb2(h, half=half, d=d, Pd=Pd):
                            return h.matmul(Pd[:, 0:512], B2R[0:1, d * 128:(d + 1) * 128],
                                            ROWSBF[0:1, 2 * TOK + half * 512: 2 * TOK + half * 512 + 512],
                                            start=False, stop=True)
                        pm = P.step("pe", fb2, sem="pe")
                    P.want("dve", pm)
                    xr_sl = c.XR[:, d * TOK + half * 512: d * TOK + half * 512 + 512]
                    em = P.step("dve", lambda h, Pd=Pd, xr_sl=xr_sl: h.tensor_tensor(xr_sl, Pd[:, 0:512], xr_sl, OP.add), sem="dve")
                    pd_free[slot] = [em]
                    if sup == 3 and half == 1:
                        c.xr_chunk[d] = [em]
                c.stage_guard[f"wstB{wbuf}"] = [("pe", P.cnt["pe"])]
            x_free = {0: pd_free[0], 1: pd_free[1]}
            y_free = {0: pd_free[2], 1: pd_free[3]}
            mixw_guard = [("pe", P.cnt["pe"])]
        c.stage_guard["b2r"] = [("pe", P.cnt["pe"])]
        c.xr_ready = [("dve", P.cnt["dve"])]
        c.xr_dirty = True
        c.psum_guard = [("dve", P.cnt["dve"]), ("act", P.cnt["act"])]
        P.full_barrier()


PAIRS = [[0, 1], [2, 3], [4, 5], [6, 7]]


def emit_attn_pre(c, si, kind):
    import contextlib
    P, nc, dr = c.P, c.nc, c.dr
    tag = f"p{si}_"
    P.full_barrier()
    c.pre_body_pe = ("pe", P.cnt["pe"])
    if not hasattr(c, "xng_free"):
        c.xng_free = {g: [] for g in range(4)}
        c.xngo_free = {g: [] for g in range(4)}
    c.gather_ms = {}
    c.gather_want = {}
    if c.xr_dirty:
        emit_spill_x(c)
    with contextlib.ExitStack() as es:
        sb = lambda n, s, d: es.enter_context(nc.sbuf_tensor(tag + n, list(s), d))
        XBF = [sb(f"XBF{i}", (128, 512), BF16) for i in range(2)]
        SQS = [sb(f"SQS{i}", (128, 512), BF16) for i in range(2)]
        _ts = sb("TS", (128, 512), F32)
        TS = [_ts, _ts]
        XNS = [sb(f"XNS{i}", (128, 512), BF16) for i in range(4)]
        psA, psB = c.PSF[:, 0:512], c.PSF[:, 512:1024]
        psC, psD = c.PSF[:, 1024:1536], c.PSF[:, 1536:2048]
        P.want("pe", *c.psum_guard)
        c.psum_guard = []
        nstage = [0]
        xns_last = {}

        def xn_sink(kc, half, t_ms, t_src, psDh):
            buf = (kc % 2) * 2 + half
            if buf in xns_last:
                P.want("dve", xns_last[buf])
            dm = P.step("dve", lambda h, d=XNS[buf], s=t_src, psDh=psDh: h.tensor_tensor(
                d[:, 0:512], s[:, 0:512], psDh[:, 0:512], OP.add), sem="dve")
            g = kc // 4
            P.want("sp", dm, *(c.xng_free[g] if (kc % 4 == 0 and half == 0) else []))
            ms = P.step("sp", lambda h, kc=kc, half=half, buf=buf, g=g: h.dma_start(
                out=c.xn_g[g][(kc % 4) * 128:(kc % 4 + 1) * 128, half * 512: half * 512 + 512],
                in_=XNS[buf][:, 0:512]), sem=f"xg{buf}", amt=16)
            xns_last[buf] = ms
            nstage[0] += 1
            if kc % 4 == 3 and half == 1:
                c.gather_want[g] = [(f"xg{b}", P.cnt[f"xg{b}"]) for b in range(4)]
            return dm

        emit_ln(c, psA, psB, psC, psD, xn_sink, SQS, XBF, TS)
        c.psum_guard = [("dve", P.cnt["dve"]), ("act", P.cnt["act"])]
        if not c.xr_dirty:
            c.xnf_region_guard = [("dve", P.cnt["dve"])]


def emit_attn_body(c, si, kind):
    import contextlib
    P, nc, dr = c.P, c.nc, c.dr
    tag = f"b{si}_"
    deltanet = (kind == "attn_b")
    pre = "b" if deltanet else "a"
    with contextlib.ExitStack() as es:
        sb = lambda n, s, d: es.enter_context(nc.sbuf_tensor(tag + n, list(s), d))
        XNF = c.XNF
        KG = sb("KG", (128, 4 * 2048), BF16)
        STEXP = sb("STEXP", (128, 16 * 512), BF16)
        SH = STEXP[:, 0:2048]
        RT = STEXP[:, 2048:4096]
        COSF = STEXP[:, 4096:6144]
        SINF = STEXP[:, 6144:8192]
        VG = sb("VG", (128, 16 * 512), BF16)
        QG = KG if deltanet else sb("QG", (128, 4 * 2048), BF16)
        KBG = sb("KBG", (128, 4 * 2048), BF16) if deltanet else None


        _wst = sb("WST0", (128, 16 * 512), BF16)
        WST = [_wst, _wst]

        wfirst = pre + ("Wk" if deltanet else "Wq")
        pre_widx = 1 if deltanet else 0
        pre_sem = f"wA{pre_widx % 2}"
        P.want("po", c.pre_body_pe,
               *(c.stage_guard.get("bodyW0") or []), *(c.stage_guard.get("bodyW1") or []))
        P.step("po", lambda h: h.dma_start(
            out=_wst[:, :].rearrange("p (k q) -> p k q", k=KC),
            in_=dr[wfirst][:, 0:4 * 128].rearrange("(k p) q -> p k q", p=128)), sem=pre_sem, amt=16)
        prestage_ms = (pre_sem, P.cnt[pre_sem])

        for g in range(4):
            P.want("po", *c.gather_want[g], *c.xngo_free[g])
            cm = P.step("po", lambda h, g=g: h.collective_compute(
                "AllGather", OP.bypass, replica_groups=PAIRS,
                ins=[c.xn_g[g].ap().opt()], outs=[c.xn_go[g].ap().opt()]), sem=f"cc{g}", amt=1)
            c.gather_ms[g] = cm
            c.xng_free[g] = [cm]

        if not hasattr(c, "xb_free"):
            c.xb_free = {}
        xb_ms = {}
        for g in range(4):
            P.want("sp", c.gather_ms[g], *getattr(c, "xnf_region_guard", []), *c.xb_free.get(g, []))
            for ci in range(4):
                kc = g * 4 + ci
                P.step("sp", lambda h, g=g, ci=ci, kc=kc: h.dma_start(
                    out=XNF[:, kc * 2048: kc * 2048 + 1024],
                    in_=c.xn_go[g][ci * 128:(ci + 1) * 128, :]), sem=f"xb{g}", amt=16)
                P.step("sp", lambda h, g=g, ci=ci, kc=kc: h.dma_start(
                    out=XNF[:, kc * 2048 + 1024: kc * 2048 + 2048],
                    in_=c.xn_go[g][512 + ci * 128: 512 + ci * 128 + 128, :]), sem=f"xb{g}", amt=16)
            xb_ms[g] = (f"xb{g}", P.cnt[f"xb{g}"])

        ctx_dma_ms = []
        attn_done_per_group = []
        BETST = sb("BETST", (128, 64), BF16) if deltanet else None
        CTXF = sb("CTXF", (128, 2048), BF16)
        KEEP = sb("KEEP", (128, 2048), BF16)
        BCS = sb("BCS", (128, 512), BF16)
        RECB = sb("RECB", (1, 512), BF16)
        RSROW = CTXF[0:1, 0:2048]
        BETA = CTXF[32:36, 0:2048] if deltanet else None
        for G in range(2):
            P.full_barrier()
            P.want("sp", ("pe", P.cnt["pe"]), ("act", P.cnt["act"]), ("dve", P.cnt["dve"]))
            P.step("sp", lambda h: h.dma_start(out=COSF, in_=dr["cosf"]), sem="cst", amt=16)
            P.step("sp", lambda h: h.dma_start(out=SINF, in_=dr["sinf"]), sem="cst", amt=16)
            cst1 = cst2 = ("cst", P.cnt["cst"])
            P.want("dve", ("pe", P.cnt["pe"]), cst1)
            if True:
                QA = c.PSF[:, :]
                QB = c.PSG[:, :]
                P.want("pe", *c.psum_guard)
                c.psum_guard = []

                def stage(widx, wname, cols, slot_guard):
                    sem = f"wA{widx % 2}"
                    P.want("po", *slotg[0], *slotg[1])
                    P.step("po", lambda h, wname=wname, cols=cols, widx=widx: h.dma_start(
                        out=WST[widx % 2][:, :].rearrange("p (k q) -> p k q", k=KC),
                        in_=dr[wname][:, cols[0]:cols[1]].rearrange("(k p) q -> p k q", p=128)), sem=sem, amt=16)
                    return (sem, P.cnt[sem])

                cols = (G * 4 * 128, (G + 1) * 4 * 128)
                psg = {0: list(c.psum_guard), 1: list(c.psum_guard)}
                c.psum_guard = []
                slotg = {0: list(c.stage_guard.get("bodyW0") or []),
                         1: list(c.stage_guard.get("bodyW1") or [])}

                def produce_fm(bufT, widx, wms):
                    evs = []
                    for hh in range(4):
                        pi = hh % 2
                        PSU = QA if pi == 0 else QB
                        P.want("pe", wms, *psg[pi])
                        pm = None
                        for kc in range(KC):
                            if kc % 4 == 0:
                                P.want("pe", xb_ms[kc // 4])
                            for n in range(4):
                                def fmm(h, hh=hh, kc=kc, n=n, PSU=PSU, widx=widx,
                                        st=(kc == 0), sp=(kc == KC - 1)):
                                    return h.matmul(PSU[:, n * 512:(n + 1) * 512],
                                                    WST[widx % 2][:, kc * 512 + hh * 128: kc * 512 + (hh + 1) * 128],
                                                    XNF[:, kc * 2048 + n * 512: kc * 2048 + (n + 1) * 512],
                                                    start=st, stop=sp)
                                pm = P.step("pe", fmm, sem="pe" if (kc == KC - 1 and n == 3) else None)
                        P.want("act", pm)
                        am = P.step("act", lambda h, hh=hh, PSU=PSU, bufT=bufT: h.activation(
                            bufT[:, hh * 2048:(hh + 1) * 2048], PSU[:, :], AF.Copy), sem="act")
                        psg[pi] = [am]
                        evs.append(am)
                    slotg[widx % 2] = [("pe", P.cnt["pe"])]
                    return evs

                def produce_v(wms):
                    evs = []
                    for t4 in range(4):
                        pi = t4 % 2
                        PSU = QA if pi == 0 else QB
                        P.want("pe", wms, *psg[pi])
                        pm = None
                        for ti in range(4):
                            tile = t4 * 4 + ti
                            for kc in range(KC):
                                if kc % 4 == 0:
                                    P.want("pe", xb_ms[kc // 4])
                                def fmv(h, tile=tile, ti=ti, kc=kc, PSU=PSU, st=(kc == 0)):
                                    return h.matmul(PSU[:, ti * 512:(ti + 1) * 512],
                                                    XNF[:, kc * 2048 + tile * 128: kc * 2048 + (tile + 1) * 128],
                                                    WST[0][:, kc * 512:(kc + 1) * 512],
                                                    start=st, stop=(kc == KC - 1))
                                pm = P.step("pe", fmv, sem="pe" if (kc == KC - 1 and ti == 3) else None)
                        P.want("act", pm)
                        am = P.step("act", lambda h, t4=t4, PSU=PSU: h.activation(
                            VG[:, t4 * 2048:(t4 + 1) * 2048], PSU[:, :], AF.Copy), sem="act")
                        psg[pi] = [am]
                        evs.append(am)
                    slotg[0] = [("pe", P.cnt["pe"])]
                    return evs

                if not deltanet:
                    ms_q = prestage_ms if G == 0 else stage(0, pre + "Wq", cols, None)
                    q_ev = produce_fm(QG, 0, ms_q)
                    ms_k = stage(1, pre + "Wk", cols, None)
                    k_ev = produce_fm(KG, 1, ms_k)
                    ms_v = stage(0, pre + "Wv", cols, None)
                    v_ev = produce_v(ms_v)
                else:
                    ms_k = prestage_ms if G == 0 else stage(1, pre + "Wk", cols, None)
                    k_ev = produce_fm(KG, 1, ms_k)
                    ms_v = stage(0, pre + "Wv", cols, None)
                    v_ev = produce_v(ms_v)

                if deltanet:
                    P.want("po", *c.stage_guard.get("bodyBET") or [])
                    bms = P.step("po", lambda h, G=G, BETST=BETST: h.dma_start(
                        out=BETST[:, :].rearrange("p (k q) -> p k q", k=KC),
                        in_=dr["bWb"][:, G * 4:(G + 1) * 4].rearrange("(k p) q -> p k q", p=128)), sem="wmisc", amt=16)
                    P.want("pe", bms, *v_ev)
                    for kc in range(KC):
                        if kc % 4 == 0:
                            P.want("pe", xb_ms[kc // 4])
                        for n in range(4):
                            def fmb(h, kc=kc, n=n, st=(kc == 0)):
                                return h.matmul(QA[0:4, n * 512:(n + 1) * 512],
                                                BETST[:, kc * 4:(kc + 1) * 4],
                                                XNF[:, kc * 2048 + n * 512: kc * 2048 + (n + 1) * 512],
                                                start=st, stop=(kc == KC - 1))
                            pm = P.step("pe", fmb, sem="pe" if (kc == KC - 1 and n == 3) else None)
                    P.want("act", pm)
                    bsig = P.step("act", lambda h, BETA=BETA: h.activation(BETA[0:4, :], QA[0:4, :], AF.Sigmoid, scale=1.0), sem="act")
                    P.want("pe", bsig)
                    c.stage_guard["bodyBET"] = [("pe", P.cnt["pe"])]

                pairmask = []
                for i in range(16):
                    pairmask += [2 * i + 1, 2 * i]
                rope_done = []
                for hh in range(4):
                    kh = KG[:, hh * 2048:(hh + 1) * 2048]
                    if deltanet:
                        P.want("dve", k_ev[hh])
                        d1 = P.step("dve", lambda h, kh=kh: h.tensor_tensor(SH[:, :], kh, kh, OP.mult), sem="dve")
                        P.want("pe", d1)
                        for n in range(4):
                            pm = P.step("pe", lambda h, n=n, st=True: h.matmul(
                                QB[0:1, n * 512:(n + 1) * 512], c.ones_col_bf[:, :],
                                SH[:, n * 512:(n + 1) * 512], start=True, stop=True),
                                sem="pe" if n == 3 else None)
                        P.want("act", pm)
                        P.step("act", lambda h: h.activation(RSROW[0:1, :], QB[0:1, :], AF.Ln, bias=c.epsl2[0:1, 0:1]), sem="act")
                        d2 = P.step("act", lambda h: h.activation(RSROW[0:1, :], RSROW[0:1, :], AF.Exp, scale=-0.5), sem="act")
                        P.want("pe", d2)
                        for n in range(4):
                            pm = P.step("pe", lambda h, n=n: h.matmul(
                                QA[:, n * 512:(n + 1) * 512], c.ones_row_bf[:, :],
                                RSROW[0:1, n * 512:(n + 1) * 512], start=True, stop=True),
                                sem="pe" if n == 3 else None)
                        P.want("dve", pm)
                        d3 = P.step("dve", lambda h, kh=kh: h.tensor_tensor(kh, kh, QA[:, :], OP.mult), sem="dve")
                        P.want("pe", d3)
                    P.want("dve", k_ev[hh], cst1, cst2)
                    r1 = P.step("dve", lambda h, kh=kh: h.stream_shuffle(SH[:, :], kh, pairmask), sem="dve")
                    r2 = P.step("dve", lambda h, kh=kh: h.tensor_tensor(RT[:, :], kh, COSF[:, :], OP.mult), sem="dve")
                    r3 = P.step("dve", lambda h: h.tensor_tensor(SH[:, :], SH[:, :], SINF[:, :], OP.mult), sem="dve")
                    r4 = P.step("dve", lambda h, kh=kh: h.tensor_tensor(kh, RT[:, :], SH[:, :], OP.add), sem="dve")
                    rope_done.append(r4)
                    if deltanet:
                        P.want("pe", r4)
                        for n in range(4):
                            pm = P.step("pe", lambda h, hh=hh, n=n, BETA=BETA: h.matmul(
                                QB[0:1, n * 512:(n + 1) * 512], c.oh32[32:36, hh:hh + 1],
                                BETA[0:4, n * 512:(n + 1) * 512], start=True, stop=True),
                                sem="pe" if n == 3 else None)
                        P.want("act", pm)
                        a2 = P.step("act", lambda h: h.activation(RSROW[0:1, :], QB[0:1, :], AF.Copy, scale=2.0), sem="act")
                        P.want("pe", a2)
                        for n in range(4):
                            pm = P.step("pe", lambda h, n=n: h.matmul(
                                QA[:, n * 512:(n + 1) * 512], c.ones_row_bf[:, :],
                                RSROW[0:1, n * 512:(n + 1) * 512], start=True, stop=True),
                                sem="pe" if n == 3 else None)
                        P.want("dve", pm)
                        d5 = P.step("dve", lambda h, hh=hh, kh=kh: h.tensor_tensor(
                            KBG[:, hh * 2048:(hh + 1) * 2048], kh, QA[:, :], OP.mult), sem="dve")
                        P.want("pe", d5)
                if not deltanet:
                    for hh in range(4):
                        qh = QG[:, hh * 2048:(hh + 1) * 2048]
                        P.want("dve", q_ev[hh])
                        P.step("dve", lambda h, qh=qh: h.stream_shuffle(SH[:, :], qh, pairmask), sem="dve")
                        P.step("dve", lambda h, qh=qh: h.tensor_tensor(RT[:, :], qh, COSF[:, :], OP.mult), sem="dve")
                        P.step("dve", lambda h: h.tensor_tensor(SH[:, :], SH[:, :], SINF[:, :], OP.mult), sem="dve")
                        P.step("dve", lambda h, qh=qh: h.tensor_tensor(qh, RT[:, :], SH[:, :], OP.add), sem="dve")
                c.psum_guard = [("dve", P.cnt["dve"]), ("act", P.cnt["act"]), ("pe", P.cnt["pe"])]
                c.stage_guard["bodyW0"] = [("pe", P.cnt["pe"])]
                c.stage_guard["bodyW1"] = [("pe", P.cnt["pe"])]

            P.full_barrier()
            if deltanet:
                emit_scan(c, tag, G, KG, KBG, VG)
                P.full_barrier()

            if deltanet:
                if True:
                    WSQ = WST[0]
                    QA = c.PSF[:, :]
                    QB = c.PSG[:, :]
                    P.want("pe", *c.psum_guard)
                    c.psum_guard = []
                    P.want("po", *(c.stage_guard.get("bodyWSQ") or []), *(c.stage_guard.get("bodyW0") or []))
                    wq_ms = P.step("po", lambda h, G=G, WSQ=WSQ: h.dma_start(
                        out=WSQ[:, :].rearrange("p (k q) -> p k q", k=KC),
                        in_=dr["bWq"][:, G * 512:(G + 1) * 512].rearrange("(k p) q -> p k q", p=128)), sem="wmisc", amt=16)
                    qev = []
                    for hh in range(4):
                        PSU = QA if hh % 2 == 0 else QB
                        P.want("pe", wq_ms, *(qev[hh - 2:hh - 1] if hh >= 2 else []))
                        for kc in range(KC):
                            if kc % 4 == 0:
                                P.want("pe", xb_ms[kc // 4])
                            for n in range(4):
                                def fmq(h, hh=hh, kc=kc, n=n, PSU=PSU, st=(kc == 0)):
                                    return h.matmul(PSU[:, n * 512:(n + 1) * 512],
                                                    WSQ[:, kc * 512 + hh * 128: kc * 512 + (hh + 1) * 128],
                                                    XNF[:, kc * 2048 + n * 512: kc * 2048 + (n + 1) * 512],
                                                    start=st, stop=(kc == KC - 1))
                                pm = P.step("pe", fmq, sem="pe" if (kc == KC - 1 and n == 3) else None)
                        P.want("act", pm)
                        am = P.step("act", lambda h, hh=hh, PSU=PSU: h.activation(
                            KG[:, hh * 2048:(hh + 1) * 2048], PSU[:, :], AF.Copy), sem="act")
                        qev.append(am)
                    pairmask = []
                    for i in range(16):
                        pairmask += [2 * i + 1, 2 * i]
                    for hh in range(4):
                        qh = KG[:, hh * 2048:(hh + 1) * 2048]
                        P.want("dve", qev[hh])
                        P.step("dve", lambda h, qh=qh: h.stream_shuffle(SH[:, :], qh, pairmask), sem="dve")
                        P.step("dve", lambda h, qh=qh: h.tensor_tensor(RT[:, :], qh, COSF[:, :], OP.mult), sem="dve")
                        P.step("dve", lambda h: h.tensor_tensor(SH[:, :], SH[:, :], SINF[:, :], OP.mult), sem="dve")
                        P.step("dve", lambda h, qh=qh: h.tensor_tensor(qh, RT[:, :], SH[:, :], OP.add), sem="dve")
                    c.psum_guard = [("dve", P.cnt["dve"]), ("act", P.cnt["act"])]
                    c.stage_guard["bodyWSQ"] = [("pe", P.cnt["pe"])]
                P.full_barrier()

            if DEV_TAPS and G == 0:
                P.full_barrier()
                for tp_ap, buf in ((c.tapA, VG), (c.tapB, KBG if deltanet else KG)):
                    with nc.sbuf_tensor(tag + f"tp{id(tp_ap) % 97}", [128, 8192], F32) as TPS:
                        for j in range(8):
                            P.step("dve", lambda h, TPS=TPS, buf=buf, j=j: h.tensor_copy(TPS[:, j * 1024:(j + 1) * 1024], buf[:, j * 1024:(j + 1) * 1024]), sem="dve")
                        P.want("sp", ("dve", P.cnt["dve"]))
                        P.step("sp", lambda h, TPS=TPS, tp_ap=tp_ap: h.dma_start(out=tp_ap, in_=TPS[:, :]), sem="cst", amt=16)
                        P.want("dve", ("cst", P.cnt["cst"]))
                P.full_barrier()
            if DEV_TAPS and G == 0 and deltanet:
                for tp_ap, buf in ((c.tapC, KG),):
                    with nc.sbuf_tensor(tag + "tpq", [128, 8192], F32) as TPS:
                        for j in range(8):
                            P.step("dve", lambda h, TPS=TPS, buf=buf, j=j: h.tensor_copy(TPS[:, j * 1024:(j + 1) * 1024], buf[:, j * 1024:(j + 1) * 1024]), sem="dve")
                        P.want("sp", ("dve", P.cnt["dve"]))
                        P.step("sp", lambda h, TPS=TPS, tp_ap=tp_ap: h.dma_start(out=tp_ap, in_=TPS[:, :]), sem="cst", amt=16)
                        P.want("dve", ("cst", P.cnt["cst"]))
                P.full_barrier()
            if True:
                P.want("sp", ("pe", P.cnt["pe"]), ("act", P.cnt["act"]), ("dve", P.cnt["dve"]))
                kp_ms = P.step("sp", lambda h: h.dma_start(out=KEEP[:, :], in_=dr["keep"]), sem="cst", amt=16)
                PSC = c.PSF[:, :]
                PCX = c.PSG[:, 0:512]
                PRS = c.PSG[:, 512:1024]
                PBC = c.PSG[:, 1024:1536]
                P.want("pe", *c.psum_guard)
                c.psum_guard = []
                kbuf = KBG if deltanet else KG
                qbuf = KG if deltanet else QG
                ctx_prev = None
                tail = {"fn": None}
                head_ctx = {}
                for hh in range(4):
                    qh = qbuf[:, hh * 2048:(hh + 1) * 2048]
                    for s in range(4):
                        exp_ms = []
                        P.want("act", ("pe", P.cnt["pe"]), ("dve", P.cnt["dve"]))
                        half_banks = [PSC[:, 0:1024], PSC[:, 1024:2048]]
                        exp_prev = {0: None, 1: None}
                        for bidx2 in range(2 * (s + 1)):
                            hb = bidx2 % 2
                            PH = half_banks[hb]
                            if exp_prev[hb] is not None:
                                P.want("pe", exp_prev[hb])
                            for j in range(2):
                                kt = bidx2 * 2 + j
                                def fsc(h, kt=kt, j=j, s=s, hh=hh, qh=qh, PH=PH):
                                    return h.matmul(PH[:, j * 512:(j + 1) * 512],
                                                    kbuf[:, hh * 2048 + kt * 128: hh * 2048 + (kt + 1) * 128],
                                                    qh[:, s * 512:(s + 1) * 512], start=True, stop=True)
                                pm = P.step("pe", fsc, sem="pe" if j == 1 else None)
                            P.want("act", pm)
                            am = P.step("act", lambda h, bidx2=bidx2, PH=PH: h.activation(
                                STEXP[:, bidx2 * 1024:(bidx2 + 1) * 1024], PH[:, :], AF.Exp,
                                scale=SCALE_ATT, bias=c.neg8[:, 0:1]), sem="act")
                            exp_ms.append(am)
                            exp_prev[hb] = am
                        P.want("dve", exp_ms[-1], kp_ms)
                        mk = None
                        for j0 in range(4):
                            kt = 4 * s + j0
                            mk = P.step("dve", lambda h, kt=kt, j0=j0: h.tensor_tensor(
                                STEXP[:, kt * 512:(kt + 1) * 512], STEXP[:, kt * 512:(kt + 1) * 512],
                                KEEP[:, j0 * 512:(j0 + 1) * 512], OP.mult), sem="dve")
                        if tail["fn"] is not None:
                            ctx_prev = tail["fn"]()
                            tail["fn"] = None
                        P.want("pe", mk, *exp_ms, *( [ctx_prev] if ctx_prev else [] ))
                        nkt = 4 * s + 4
                        for kt in range(nkt):
                            def frs(h, kt=kt, st=(kt == 0), sp=(kt == nkt - 1)):
                                return h.matmul(PRS[0:1, 0:512], c.ones_col_bf[:, :],
                                                STEXP[:, kt * 512:(kt + 1) * 512], start=st, stop=sp)
                            P.step("pe", frs, sem="pe" if kt == nkt - 1 else None)
                        for kt in range(nkt):
                            def fav(h, kt=kt, hh=hh, st=(kt == 0), sp=(kt == nkt - 1)):
                                return h.matmul(PCX[:, 0:512],
                                                VG[:, kt * 512 + hh * 128: kt * 512 + (hh + 1) * 128],
                                                STEXP[:, kt * 512:(kt + 1) * 512], start=st, stop=sp)
                            av = P.step("pe", fav, sem="pe" if kt == nkt - 1 else None)

                        def mk_tail(hh=hh, s=s, av=av, G=G):
                            def emit_tail():
                                P.want("act", av)
                                P.step("act", lambda h: h.activation(RECB[0:1, 0:512], PRS[0:1, 0:512], AF.Ln), sem="act")
                                rc = P.step("act", lambda h: h.activation(RECB[0:1, 0:512], RECB[0:1, 0:512], AF.Exp, scale=-1.0), sem="act")
                                P.want("pe", rc)
                                bc = P.step("pe", lambda h: h.matmul(PBC[:, 0:512], c.ones_row_bf[:, :], RECB[0:1, 0:512], start=True, stop=True), sem="pe")
                                P.want("act", bc)
                                ab = P.step("act", lambda h: h.activation(BCS[:, 0:512], PBC[:, :], AF.Copy), sem="act")
                                P.want("dve", ab, av)
                                cp = P.step("dve", lambda h, s=s: h.tensor_tensor(
                                    CTXF[:, s * 512:(s + 1) * 512], PCX[:, 0:512], BCS[:, 0:512], OP.mult), sem="dve")
                                if s == 3:
                                    gh = G * 4 + hh
                                    P.want("sp", cp, *getattr(c, "ctxst_free", []))
                                    cm = P.step("sp", lambda h, gh=gh: h.dma_start(
                                        out=c.ctx_st[gh * 128:(gh + 1) * 128, :], in_=CTXF[:, :]), sem="cs", amt=16)
                                    ctx_dma_ms.append(cm)
                                    P.want("dve", cm)
                                return cp
                            return emit_tail
                        tail["fn"] = mk_tail()
                if tail["fn"] is not None:
                    ctx_prev = tail["fn"]()
                    tail["fn"] = None
                c.psum_guard = [("dve", P.cnt["dve"]), ("act", P.cnt["act"]), ("pe", P.cnt["pe"])]
            P.full_barrier()
        P.want("po", ("cs", P.cnt["cs"]), *getattr(c, "ctxout_free", []))
        ccm = P.step("po", lambda h: h.collective_compute(
            "AllGather", OP.bypass, replica_groups=PAIRS,
            ins=[c.ctx_st.ap().opt()], outs=[c.ctx_out.ap().opt()]), sem="ccx", amt=1)
        c.ctx_cc = ("ccx", P.cnt["ccx"])
        c.ctxst_free = [c.ctx_cc]
        for g in range(4):
            c.xb_free[g] = [("pe", P.cnt["pe"])]
        c.xngo_free = {g: [xb_ms[g]] for g in range(4)}
        P.full_barrier()


def emit_scan(c, tag, G, KG, KBG, VG):
    import contextlib
    P, nc = c.P, c.nc
    with contextlib.ExitStack() as es:
        sb = lambda n, s, d: es.enter_context(nc.sbuf_tensor(tag + f"g{G}s" + n, list(s), d))
        TJ2 = sb("TJ2", (128, 2 * 7 * 512), BF16)
        TJp = [TJ2[:, 0:7 * 512], TJ2[:, 7 * 512:14 * 512]]
        PJ = sb("PJ", (128, 512), BF16)
        SSB = sb("SSB", (128, 512), BF16)
        SSB32 = sb("SSB32", (128, 512), F32)
        TMPB = sb("TMPB", (128, 512), BF16)
        KBT2 = sb("KBT2", (128, 1024), BF16)
        KBTp = [KBT2[:, 0:512], KBT2[:, 512:1024]]
        PA = c.PSF[:, 0:512]
        PBt = c.PSF[:, 512:1024]
        PS4 = c.PSF[:, 1024:1536]
        TMA = c.PSF[:, 1536:2048]
        PSQ = c.PSG[:, 0:512]
        PTR = c.PSG[:, 512:1024].bitcast(BF16)
        TMB = c.PSG[:, 1536:2048]
        P.want("pe", *c.psum_guard)
        c.psum_guard = []
        KG4 = KG[:, :].rearrange("p (hh t) -> p hh t", hh=4)
        st = {"s_ev": None, "last_apply": None, "last_tma": None, "vn_ev": None, "pw_dve": None}

        def powers_abuild(ch):
            p = ch % 2
            TJ = TJp[p]
            if st["pw_dve"]:
                P.want("pe", st["pw_dve"])
            for hh in range(4):
                P.step("pe", lambda h, hh=hh, ch=ch: h.matmul(
                    PA[:, hh * 128:(hh + 1) * 128],
                    KG[:, hh * 2048 + ch * 128: hh * 2048 + (ch + 1) * 128],
                    KBG[:, hh * 2048 + ch * 128: hh * 2048 + (ch + 1) * 128],
                    start=True, stop=True))
            for hh in range(4):
                pma = P.step("pe", lambda h, hh=hh, ch=ch: h.matmul(
                    PBt[:, hh * 128:(hh + 1) * 128],
                    KBG[:, hh * 2048 + ch * 128: hh * 2048 + (ch + 1) * 128],
                    KG[:, hh * 2048 + ch * 128: hh * 2048 + (ch + 1) * 128],
                    start=True, stop=True), sem="pe" if hh == 3 else None)
            for hh in range(4):
                ptr = P.step("pe", lambda h, hh=hh, ch=ch: h.transpose(
                    PTR[:, 512 + hh * 128: 512 + (hh + 1) * 128],
                    KBG[:, hh * 2048 + ch * 128: hh * 2048 + (ch + 1) * 128],
                    c.ident[:, :]), sem="pe" if hh == 3 else None)
            P.want("dve", pma, ptr)
            dj = P.step("dve", lambda h: h.tensor_tensor(PJ[:, :], PA[:, 0:512], c.maskl4[:, :], OP.mult), sem="dve")
            P.step("dve", lambda h, TJ=TJ: h.tensor_tensor(TJ[:, 0:512], PBt[:, 0:512], c.masku4[:, :], OP.mult), sem="dve")
            dng = P.step("dve", lambda h, p=p: h.tensor_copy(KBTp[p][:, :], PTR[:, 512:1024]), sem="dve")
            st["pw_dve"] = dng
            return dng

        def powers_level(ch, j):
            p = ch % 2
            TJ = TJp[p]
            P.want("pe", st["pw_dve"])
            for hh in range(4):
                psq = P.step("pe", lambda h, hh=hh, j=j, TJ=TJ: h.matmul(
                    PSQ[:, hh * 128:(hh + 1) * 128],
                    TJ[:, j * 512 + hh * 128: j * 512 + (hh + 1) * 128],
                    PJ[:, hh * 128:(hh + 1) * 128], start=True, stop=True),
                    sem="pe" if hh == 3 else None)
            P.want("dve", psq)
            dpj = P.step("dve", lambda h: h.tensor_copy(PJ[:, :], PSQ[:, 0:512]), sem="dve")
            P.want("pe", dpj)
            for hh in range(4):
                ptr = P.step("pe", lambda h, hh=hh: h.transpose(
                    PTR[:, hh * 128:(hh + 1) * 128], PJ[:, hh * 128:(hh + 1) * 128],
                    c.ident[:, :]), sem="pe" if hh == 3 else None)
            P.want("dve", ptr)
            st["pw_dve"] = P.step("dve", lambda h, j=j, TJ=TJ: h.tensor_copy(TJ[:, (j + 1) * 512:(j + 2) * 512], PTR[:, 0:512]), sem="dve")

        def applies_rhs(ch):
            P.want("pe", *( [st["s_ev"]] if st["s_ev"] else [] ),
                   *( [st["last_apply"]] if st["last_apply"] else [] ))
            lm = None
            for hh in range(4):
                lm = P.step("pe", lambda h, hh=hh, ch=ch: h.matmul(
                    TMA[:, hh * 128:(hh + 1) * 128], c.ident[:, :],
                    VG[:, ch * 512 + hh * 128: ch * 512 + (hh + 1) * 128],
                    start=(hh == 0), stop=(ch == 0 and hh == 3)),
                    sem="pe" if (ch == 0 and hh == 3) else None)
            if ch > 0:
                for hh in range(4):
                    lm = P.step("pe", lambda h, hh=hh, ch=ch: h.matmul(
                        TMA[:, hh * 128:(hh + 1) * 128],
                        KG[:, hh * 2048 + ch * 128: hh * 2048 + (ch + 1) * 128],
                        SSB[:, hh * 128:(hh + 1) * 128], start=False, stop=(hh == 3)),
                        sem="pe" if hh == 3 else None)
            st["last_apply"] = lm

        def apply_level(ch, j, cur, nxt):
            p = ch % 2
            TJ = TJp[p]
            P.want("act", st["last_apply"] if st["last_apply"] else ("pe", P.cnt["pe"]))
            atm = P.step("act", lambda h, cur=cur: h.activation(TMPB[:, :], cur, AF.Copy), sem="act")
            P.want("pe", atm)
            for hh in range(4):
                P.step("pe", lambda h, hh=hh, nxt=nxt: h.matmul(
                    nxt[:, hh * 128:(hh + 1) * 128], c.ident[:, :],
                    TMPB[:, hh * 128:(hh + 1) * 128], start=(hh == 0), stop=False))
            for hh in range(4):
                lm = P.step("pe", lambda h, hh=hh, j=j, nxt=nxt, TJ=TJ: h.matmul(
                    nxt[:, hh * 128:(hh + 1) * 128],
                    TJ[:, j * 512 + hh * 128: j * 512 + (hh + 1) * 128],
                    TMPB[:, hh * 128:(hh + 1) * 128], start=False, stop=(hh == 3)),
                    sem="pe" if hh == 3 else None)
            st["last_apply"] = lm

        def vn_s(ch, cur):
            p = ch % 2
            P.want("dve", st["last_apply"])
            vn_ev = P.step("dve", lambda h, ch=ch, cur=cur: h.tensor_copy(VG[:, ch * 512:(ch + 1) * 512], cur), sem="dve")
            P.want("pe", vn_ev)
            for hh in range(4):
                psu = P.step("pe", lambda h, hh=hh, ch=ch, p=p: h.matmul(
                    PS4[:, hh * 128:(hh + 1) * 128], KBTp[p][:, hh * 128:(hh + 1) * 128],
                    VG[:, ch * 512 + hh * 128: ch * 512 + (hh + 1) * 128],
                    start=(hh == 0), stop=(hh == 3)), sem="pe" if hh == 3 else None)
            if ch < 15:
                P.want("dve", psu)
                if ch == 0:
                    P.step("dve", lambda h: h.tensor_copy(SSB32[:, :], PS4[:, 0:512]), sem="dve")
                else:
                    P.step("dve", lambda h: h.tensor_tensor(SSB32[:, :], PS4[:, 0:512], SSB32[:, :], OP.add), sem="dve")
                st["s_ev"] = P.step("dve", lambda h: h.tensor_scalar(SSB[:, :], SSB32[:, :], -1.0, None, OP.mult, OP.bypass), sem="dve")

        powers_abuild(0)
        for j in range(6):
            powers_level(0, j)
        for ch in range(16):
            applies_rhs(ch)
            if ch + 1 < 16:
                powers_abuild(ch + 1)
            cur, nxt = TMA, TMB
            for j in range(7):
                apply_level(ch, j, cur, nxt)
                if ch + 1 < 16 and j < 6:
                    powers_level(ch + 1, j)
                cur, nxt = nxt, cur
            vn_s(ch, cur)
        c.psum_guard = [("dve", P.cnt["dve"]), ("act", P.cnt["act"]), ("pe", P.cnt["pe"])]


def emit_attn_post(c, kind):
    import contextlib
    P, nc, dr = c.P, c.nc, c.dr
    pre = "b" if kind == "attn_b" else "a"
    tag = f"o{pre}_"
    P.full_barrier()
    with contextlib.ExitStack() as es:
        sb = lambda n, s, d: es.enter_context(nc.sbuf_tensor(tag + n, list(s), d))
        CTA = sb("CTA", (128, 16 * 1024), BF16)
        CTB = [sb(f"CTB{i}", (128, 1024), BF16) for i in range(2)]
        PD = [c.PSF[:, 0:512], c.PSF[:, 512:1024], c.PSF[:, 1024:1536], c.PSF[:, 1536:2048]]
        P.want("pe", *c.psum_guard)
        c.psum_guard = []
        sel_ms = []
        for cidx in range(16):
            P.want("sp", c.ctx_cc, *(sel_ms[-2:-1] if cidx >= 2 else []))
            P.step("sp", lambda h, cidx=cidx: h.dma_start(
                out=CTA[:, cidx * 1024:(cidx + 1) * 1024],
                in_=c.ctx_out[cidx * 128:(cidx + 1) * 128, 0:1024]), sem=f"wo{cidx % 2}", amt=16)
            P.step("sp", lambda h, cidx=cidx: h.dma_start(
                out=CTB[cidx % 2][:, :],
                in_=c.ctx_out[cidx * 128:(cidx + 1) * 128, 1024:2048]), sem=f"wo{cidx % 2}", amt=16)
            P.want("dve", (f"wo{cidx % 2}", P.cnt[f"wo{cidx % 2}"]))
            P.step("dve", lambda h, cidx=cidx: h.tensor_scalar(
                CTA[:, cidx * 1024:(cidx + 1) * 1024], CTA[:, cidx * 1024:(cidx + 1) * 1024],
                c.npar[:, 0:1], None, OP.mult, OP.bypass), sem="dve")
            dm = P.step("dve", lambda h, cidx=cidx: h.scalar_tensor_tensor(
                CTA[:, cidx * 1024:(cidx + 1) * 1024], CTB[cidx % 2][:, :], c.par[:, 0:1],
                CTA[:, cidx * 1024:(cidx + 1) * 1024], OP.mult, OP.add), sem="dve")
            sel_ms.append(dm)
        sel_all = ("dve", P.cnt["dve"])
        def stage_wo(d, buf):
            P.want("po", *c.stage_guard.get(f"wstB{buf}") or [])
            return P.step("po", lambda h, d=d, buf=buf: h.dma_start(
                out=c.wstB[buf][:, :].rearrange("p (q dd) -> p q dd", q=KC),
                in_=dr[pre + "Wo"][:, d * 128:(d + 1) * 128].rearrange("(q p) dd -> p q dd", p=128)), sem=f"wB{buf}", amt=16)
        wo_ms = {0: stage_wo(0, 0)}
        pd_free = {0: [], 1: [], 2: [], 3: []}
        for d in range(16):
            if d + 1 < 16:
                wo_ms[d + 1] = stage_wo(d + 1, (d + 1) % 2)
            for half in range(2):
                slot = (2 * d + half) % 4
                Pd = PD[slot]
                P.want("pe", wo_ms[d], sel_all, *pd_free[slot])
                for cidx in range(16):
                    def fwo(h, cidx=cidx, d=d, half=half, Pd=Pd, st=(cidx == 0), sp=(cidx == 15)):
                        return h.matmul(Pd[:, 0:512], c.wstB[d % 2][:, cidx * 128:(cidx + 1) * 128],
                                        CTA[:, cidx * 1024 + half * 512: cidx * 1024 + half * 512 + 512],
                                        start=st, stop=sp)
                    pm = P.step("pe", fwo, sem="pe" if cidx == 15 else None)
                P.want("dve", pm, *(c.xr_chunk[d] if half == 0 else []))
                xr_sl = c.XR[:, d * TOK + half * 512: d * TOK + half * 512 + 512]
                em = P.step("dve", lambda h, Pd=Pd, xr_sl=xr_sl: h.tensor_tensor(xr_sl, Pd[:, 0:512], xr_sl, OP.add), sem="dve")
                pd_free[slot] = [em]
                if half == 1:
                    c.xr_chunk[d] = [em]
            c.stage_guard[f"wstB{d % 2}"] = [("pe", P.cnt["pe"])]
        c.ctxout_free = [("pe", P.cnt["pe"])]
        c.xr_ready = [("dve", P.cnt["dve"])]
        c.xr_dirty = True
        c.psum_guard = [("dve", P.cnt["dve"])]
        P.full_barrier()


def kernel(**inputs):
    in_maps = host_prep(inputs)
    nc, c = build_program()
    res = run_bass_kernel_spmd(nc, in_maps, core_ids=list(range(8)))
    out = np.empty((B, S, D), np.float32)
    for core in range(8):
        b, half = core // 2, core % 2
        out[b, half * TOK:(half + 1) * TOK, :] = res.results[core]["out"].T
    return out


# version 09
# speedup vs baseline: 1.0155x; 1.0155x over previous
import numpy as np
import ml_dtypes
import concourse.bass as bass
import concourse.mybir as mybir
from concourse.alu_op_type import AluOpType as OP
from concourse.bass_utils import run_bass_kernel_spmd

F32 = mybir.dt.float32
BF16 = mybir.dt.bfloat16
AF = mybir.ActivationFunctionType
AX = mybir.AxisListType
BF = ml_dtypes.bfloat16

D = 2048
S = 2048
B = 4
TOK = 1024
H = 16
HD = 128
MYH = 8
F = 8192
E, R = 8, 16
ER = E * R
KC = D // 128
NEG = -1e30
EPS_LN = 1e-5
EPS_L2 = 1e-6
SCALE_ATT = float(1.0 / np.sqrt(HD))
CUT = 99
DEV_TAPS = False
RACE_DETECT = True

ENGS = ("pe", "act", "dve", "po", "sp")
BLOCK_ATTR = {"pe": "tensor", "act": "scalar", "dve": "vector", "po": "gpsimd", "sp": "sync"}
SEMS = ("pe", "act", "dve", "po", "cst", "xs", "wmisc",
        "wA0", "wA1", "wB0", "wB1", "wC0", "wC1",
        "xg0", "xg1", "xg2", "xg3", "xb0", "xb1", "xb2", "xb3",
        "cc0", "cc1", "cc2", "cc3", "cs", "qv", "wo0", "wo1", "ccx")


class Prog:

    def __init__(self):
        self.steps = {e: [] for e in ENGS}
        self.cnt = dict.fromkeys(SEMS, 0)
        self.lastw = {e: dict.fromkeys(SEMS, 0) for e in ENGS}
        self.pend = {e: [] for e in ENGS}

    def want(self, eng, *mss):
        for ms in mss:
            if ms is not None:
                self.pend[eng].append(ms)

    def step(self, eng, fn, sem=None, amt=1):
        if eng in ("dve", "act") and self.cnt[eng] > 0:
            self.pend[eng].append((eng, self.cnt[eng]))
        waits = []
        for (s, v) in self.pend[eng]:
            if v > self.lastw[eng][s]:
                waits.append((s, v))
                self.lastw[eng][s] = v
        self.pend[eng] = []
        ms = None
        if sem is not None:
            self.cnt[sem] += amt
            ms = (sem, self.cnt[sem])
        self.steps[eng].append((fn, (sem, amt) if sem else None, tuple(waits)))
        return ms

    def barrier_ms(self):
        return [(s, self.cnt[s]) for s in ("pe", "act", "dve", "po")]

    def full_barrier(self):
        mss = [(s, self.cnt[s]) for s in SEMS if self.cnt[s] > 0]
        for e in ENGS:
            self.want(e, *mss)

    def realize(self, block, sems):
        for eng in ENGS:
            steps = self.steps[eng]

            def make(steps):
                def body(h):
                    for fn, inc, waits in steps:
                        for (s, v) in waits:
                            h.wait_ge(sems[s], v)
                        r = fn(h)
                        if inc is not None:
                            r.then_inc(sems[inc[0]], inc[1])
                return body
            getattr(block, BLOCK_ATTR[eng])(make(steps))


def host_consts():
    c = {}
    inv = 1.0 / 10000.0 ** (np.arange(0, HD, 2, dtype=np.float64) / HD)
    ang = np.outer(inv, np.arange(S, dtype=np.float64))
    cosf = np.repeat(np.cos(ang), 2, axis=0).astype(np.float32)
    sinf = np.repeat(np.sin(ang), 2, axis=0).astype(np.float32)
    sgn = np.tile(np.array([-1.0, 1.0], np.float32), HD // 2)[:, None]
    c["cosf"] = cosf.astype(BF)
    c["sinf"] = (sinf * sgn).astype(BF)
    c["ident"] = np.eye(128, dtype=BF)
    c["ident_f"] = np.eye(128, dtype=np.float32)
    keep = np.zeros((4, 128, 512), np.float32)
    for j0 in range(4):
        keep[j0, :, (j0 + 1) * 128:] = 1.0
        keep[j0, :, j0 * 128:(j0 + 1) * 128] = np.triu(np.ones((128, 128), np.float32), 0)
    c["keep"] = np.concatenate([keep[j] for j in range(4)], axis=1).astype(BF)
    ml = -np.tril(np.ones((128, 128), np.float32), -1)
    mu = -np.triu(np.ones((128, 128), np.float32), 1)
    c["maskl4"] = np.tile(ml, (1, 4)).astype(BF)
    c["masku4"] = np.tile(mu, (1, 4)).astype(BF)
    c["E16"] = np.repeat(np.eye(E, dtype=BF), R, axis=1)
    c["onehot8"] = np.eye(8, dtype=BF)
    oh32 = np.zeros((128, 8), BF)
    oh32[32:36, 0:4] = np.eye(4, dtype=BF)
    c["oh32"] = oh32
    c["ones_row"] = np.ones((1, 128), np.float32)
    c["ones_row_bf"] = np.ones((1, 128), BF)
    c["ones_col_bf"] = np.ones((128, 1), BF)
    return c


def host_prep(inputs):
    x = np.asarray(inputs["L0_attn_a_x"], np.float32)
    perm = np.zeros(HD, np.int64)
    perm[0::2] = np.arange(64)
    perm[1::2] = np.arange(64) + 64

    aWq = np.asarray(inputs["L0_attn_a_Wq"], np.float32)
    aWk = np.asarray(inputs["L0_attn_a_Wk"], np.float32)
    aWv = np.asarray(inputs["L0_attn_a_Wv"], np.float32)
    aWo = np.asarray(inputs["L0_attn_a_Wo"], np.float32)
    bWq = np.asarray(inputs["L0_attn_b_Wq"], np.float32).reshape(D, H, HD)[:, :, perm].reshape(D, D)
    bWk = np.asarray(inputs["L0_attn_b_Wk"], np.float32).reshape(D, H, HD)[:, :, perm].reshape(D, D)
    bWv = np.asarray(inputs["L0_attn_b_Wv"], np.float32)
    bWo = np.asarray(inputs["L0_attn_b_Wo"], np.float32)
    bWb = np.asarray(inputs["L0_attn_b_Wb"], np.float32)
    Wr = np.asarray(inputs["L0_ffn_Wr"], np.float32)
    br = np.asarray(inputs["L0_ffn_br"], np.float32)
    W1 = np.asarray(inputs["L0_ffn_W1"], np.float32)
    b1 = np.asarray(inputs["L0_ffn_b1"], np.float32)
    W2 = np.asarray(inputs["L0_ffn_W2"], np.float32)
    b2 = np.asarray(inputs["L0_ffn_b2"], np.float32)
    A1s = np.ascontiguousarray(np.asarray(inputs["L0_ffn_A1"], np.float32).transpose(2, 0, 1).reshape(D, ER))
    B1s = np.ascontiguousarray(np.asarray(inputs["L0_ffn_B1"], np.float32).transpose(0, 2, 1).reshape(ER, F))
    A2s = np.ascontiguousarray(np.asarray(inputs["L0_ffn_A2"], np.float32).transpose(2, 0, 1).reshape(F, ER))
    B2s = np.ascontiguousarray(np.asarray(inputs["L0_ffn_B2"], np.float32).transpose(0, 2, 1).reshape(ER, D))

    consts = host_consts()
    in_maps = []
    for core in range(8):
        b, half = core // 2, core % 2
        hs = half * MYH * HD
        m = {
            "xT": np.ascontiguousarray(x[b].T[:, half * TOK:(half + 1) * TOK]),
            "aWq": np.ascontiguousarray(aWq[:, hs:hs + MYH * HD]),
            "aWk": np.ascontiguousarray(aWk[:, hs:hs + MYH * HD]),
            "aWv": np.ascontiguousarray(aWv[:, hs:hs + MYH * HD]),
            "aWo": aWo,
            "bWq": np.ascontiguousarray(bWq[:, hs:hs + MYH * HD]),
            "bWk": np.ascontiguousarray(bWk[:, hs:hs + MYH * HD]),
            "bWv": np.ascontiguousarray(bWv[:, hs:hs + MYH * HD]),
            "bWo": bWo,
            "bWb": np.ascontiguousarray(bWb[:, half * MYH:(half + 1) * MYH]),
            "Wr": Wr, "W1": W1, "W2": W2,
            "A1s": A1s, "B1s": B1s, "A2s": A2s, "B2s": B2s,
            "br_row": br.reshape(1, E).astype(BF),
            "b1_col": np.ascontiguousarray(b1.reshape(F // 128, 128).T),
            "b2_row": b2.reshape(1, D).astype(BF),
            "par": np.full((128, 1), float(half), np.float32),
            "npar": np.full((128, 1), float(1 - half), np.float32),
        }
        m.update(consts)
        in_maps.append(m)
    return in_maps


class Ctx:
    pass


class BitView:

    def __init__(self, t):
        self.t = t

    def __getitem__(self, idx):
        p, s = idx
        return self.t[p, s.start // 2: s.stop // 2].bitcast(BF16)


def build_program(sublayers=("attn_a", "ffn", "attn_b", "ffn")):
    import contextlib
    nc = bass.Bass("TRN2", target_bir_lowering=False, detect_race_conditions=RACE_DETECT)
    P = Prog()
    c = Ctx()
    c.nc, c.P = nc, P
    c.psum_guard = []
    c.stage_guard = {}

    dr = {}
    def din(name, shape, dt=F32):
        dr[name] = nc.dram_tensor(name, list(shape), dt, kind="ExternalInput").ap()
    din("xT", (D, TOK))
    for w in ("aWq", "aWk", "aWv", "bWq", "bWk", "bWv"):
        din(w, (D, MYH * HD))
    din("aWo", (D, D)); din("bWo", (D, D)); din("bWb", (D, MYH))
    din("Wr", (D, E)); din("W1", (D, F)); din("W2", (F, D))
    din("A1s", (D, ER)); din("B1s", (ER, F)); din("A2s", (F, ER)); din("B2s", (ER, D))
    din("br_row", (1, E), BF16); din("b1_col", (128, F // 128)); din("b2_row", (1, D), BF16)
    din("par", (128, 1)); din("npar", (128, 1))
    din("cosf", (128, S), BF16); din("sinf", (128, S), BF16)
    din("ident", (128, 128), BF16); din("ident_f", (128, 128))
    din("keep", (128, 2048), BF16)
    din("maskl4", (128, 512), BF16); din("masku4", (128, 512), BF16)
    din("E16", (E, 128), BF16); din("onehot8", (E, E), BF16); din("oh32", (128, E), BF16); din("ones_row", (1, 128)); din("ones_row_bf", (1, 128), BF16)
    din("ones_col_bf", (128, 1), BF16)
    c.dr = dr
    out_ext = nc.dram_tensor("out", [D, TOK], F32, kind="ExternalOutput").ap()
    if DEV_TAPS:
        c.tapA = nc.dram_tensor("tapA", [128, 8192], F32, kind="ExternalOutput").ap()
        c.tapB = nc.dram_tensor("tapB", [128, 8192], F32, kind="ExternalOutput").ap()
        c.tapC = nc.dram_tensor("tapC", [128, 8192], F32, kind="ExternalOutput").ap()
        c.tapD = nc.dram_tensor("tapD", [128, 8192], F32, kind="ExternalOutput").ap()

    c.xspill = nc.dram_tensor("xspill", [128, 16 * TOK], F32)
    c.xn_g = [nc.dram_tensor(f"xn_g{g}", [4 * 128, TOK], BF16) for g in range(4)]
    c.xn_go = [nc.dram_tensor(f"xn_go{g}", [8 * 128, TOK], BF16) for g in range(4)]
    c.ctx_st = nc.dram_tensor("ctx_st", [MYH * 128, S], BF16)
    c.ctx_out = nc.dram_tensor("ctx_out", [H * 128, S], BF16)

    es = contextlib.ExitStack()
    with es:
        sb = lambda name, shape, dt: es.enter_context(nc.sbuf_tensor("s_" + name, list(shape), dt))
        c.ident = sb("ident", (128, 128), BF16)
        c.ident_f = sb("identf", (128, 128), F32)
        c.maskl4 = sb("maskl4", (128, 512), BF16)
        c.masku4 = sb("masku4", (128, 512), BF16)
        c.onehot8 = sb("onehot8", (E, E), BF16)
        c.oh32 = sb("oh32", (128, E), BF16)
        c.neg8 = sb("neg8", (128, 1), F32)
        c.E16 = sb("E16c", (E, 128), BF16)
        c.ones_row = sb("ones_row", (1, 128), F32)
        c.ones_row_bf = sb("ones_row_b", (1, 128), BF16)
        c.ones_col_bf = sb("ones_col_b", (128, 1), BF16)
        c.par = sb("par", (128, 1), F32)
        c.npar = sb("npar", (128, 1), F32)
        c.b1_col = sb("b1_colS", (128, F // 128), F32)
        c.br_row = sb("br_rowS", (1, E), BF16)
        c.epsln = sb("epsln", (1, 1), F32)
        c.epsl2 = sb("epsl2", (1, 1), F32)
        c.r_row = sb("r_row", (1, 512), F32)
        c.nmr_row = sb("nmr_row", (1, 512), F32)
        c.rowscr = sb("rowscr", (1, 512), F32)
        c.rowscr2 = sb("rowscr2", (1, 512), F32)
        XU = sb("XU", (128, 16 * TOK), F32)
        c.PSF = es.enter_context(nc.psum_tensor("PSF", [128, 2048], F32))
        c.PSG = es.enter_context(nc.psum_tensor("PSG", [128, 2048], F32))
        c.XR = XU
        c.XNF = BitView(XU)
        c.wstA = [sb(f"wstA{i}", (128, 2048), BF16) for i in range(2)]
        c.wstB = [sb(f"wstB{i}", (128, 2048), BF16) for i in range(2)]
        c.wstC = [sb(f"wstC{i}", (128, 256), BF16) for i in range(2)]

        emit_p0(c)
        c.xr_sources = [c.xspill_ms]

        i = 0
        pending_wo = None
        first = True
        while i <= len(sublayers):
            emit_restore_x(c, first=first)
            if pending_wo is not None:
                emit_attn_post(c, pending_wo)
                pending_wo = None
            while i < len(sublayers) and sublayers[i] == "ffn":
                emit_ffn(c, i)
                i += 1
            if i < len(sublayers):
                kind = sublayers[i]
                emit_attn_pre(c, i, kind)
            else:
                emit_store_out(c, out_ext)
                break
            emit_attn_body(c, i, sublayers[i])
            pending_wo = sublayers[i]
            i += 1
            first = False

        sems = {}
        with contextlib.ExitStack() as es2:
            for s in SEMS:
                sems[s] = es2.enter_context(nc.semaphore(s))
            with nc.Block() as block:
                P.realize(block, sems)
    return nc, c


def emit_p0(c):
    P, dr = c.P, c.dr
    loads = [
        (c.ident, "ident"), (c.ident_f, "ident_f"),
        (c.maskl4, "maskl4"), (c.masku4, "masku4"), (c.onehot8, "onehot8"), (c.oh32, "oh32"),
        (c.E16, "E16"), (c.ones_row, "ones_row"), (c.ones_row_bf, "ones_row_bf"),
        (c.ones_col_bf, "ones_col_bf"), (c.par, "par"), (c.npar, "npar"),
        (c.b1_col, "b1_col"), (c.br_row, "br_row"),
    ]
    for t, name in loads:
        P.step("sp", lambda h, t=t, name=name: h.dma_start(out=t[:, :], in_=dr[name]), sem="cst", amt=16)
    P.step("po", lambda h: h.memset(c.epsln[0:1, 0:1], EPS_LN), sem="po")
    P.step("po", lambda h: h.memset(c.epsl2[0:1, 0:1], EPS_L2), sem="po")
    P.step("po", lambda h: h.memset(c.neg8[:, 0:1], -8.0), sem="po")
    def fn(h):
        return h.dma_start(out=c.xspill[:, :].rearrange("p (c t) -> p c t", c=KC), in_=dr["xT"].rearrange("(c p) t -> p c t", p=128))
    P.step("sp", fn, sem="xs", amt=16)
    c.consts_ms = ("cst", P.cnt["cst"])
    c.xspill_ms = ("xs", P.cnt["xs"])


def emit_restore_x(c, first=False):
    P, dr = c.P, c.dr
    P.want("sp", *c.xr_sources)
    c.xr_chunk = {}
    for kc in range(KC):
        if first:
            P.step("sp", lambda h, kc=kc: h.dma_start(
                out=c.XR[:, kc * TOK:(kc + 1) * TOK],
                in_=dr["xT"][kc * 128:(kc + 1) * 128, :]), sem="xs", amt=16)
        else:
            P.step("sp", lambda h, kc=kc: h.dma_start(
                out=c.XR[:, kc * TOK:(kc + 1) * TOK],
                in_=c.xspill[:, kc * TOK:(kc + 1) * TOK]), sem="xs", amt=16)
        c.xr_chunk[kc] = [("xs", P.cnt["xs"])]
    c.xr_ready = [("xs", P.cnt["xs"])]
    c.xr_dirty = False


def emit_spill_x(c):
    P = c.P
    for kc in range(KC):
        P.want("sp", *c.xr_chunk[kc])
        P.step("sp", lambda h, kc=kc: h.dma_start(
            out=c.xspill[:, kc * TOK:(kc + 1) * TOK],
            in_=c.XR[:, kc * TOK:(kc + 1) * TOK]), sem="xs", amt=16)
    c.xr_sources = [("xs", P.cnt["xs"])]
    c.xnf_region_guard = [("xs", P.cnt["xs"])]


def emit_store_out(c, out_ext):
    P = c.P
    P.want("sp", *c.xr_ready)
    P.step("sp", lambda h: h.dma_start(out=out_ext.rearrange("(c p) t -> p c t", p=128), in_=c.XR[:, :].rearrange("p (c t) -> p c t", c=KC)),
           sem="xs", amt=16)
    fin = ("xs", P.cnt["xs"])
    P.want("po", fin)
    P.step("po", lambda h: h.memset(c.ones_row_bf[0:1, 0:1], 1.0))



def emit_ln(c, psA, psB, psC, psD, xn_sink, sq_scr, xbf_scr, t_scr):
    P = c.P
    hold_pe = []
    stat_ms = {}
    for half in range(2):
        co = half * 512
        pm2 = None
        pm_hist = {}
        for kc in range(KC):
            xr = c.XR[:, kc * TOK + co: kc * TOK + co + 512]
            P.want("dve", *c.xr_chunk[kc])
            if kc >= 2:
                P.want("dve", pm_hist[kc - 2][0], pm_hist[kc - 2][1])
            m1 = P.step("dve", lambda h, xr=xr, d=xbf_scr[kc % 2]: h.tensor_copy(d[:, 0:512], xr), sem="dve")
            m2 = P.step("dve", lambda h, s=xbf_scr[kc % 2], d=sq_scr[kc % 2]: h.tensor_tensor(d[:, 0:512], s[:, 0:512], s[:, 0:512], OP.mult), sem="dve")
            P.want("pe", m1, m2)
            if kc == 0:
                P.want("pe", *hold_pe)
            def f1(h, s=xbf_scr[kc % 2], st=(kc == 0), sp=(kc == KC - 1)):
                return h.matmul(psA[0:1, 0:512], c.ones_col_bf[:, :], s[:, 0:512], start=st, stop=sp)
            def f2(h, s=sq_scr[kc % 2], st=(kc == 0), sp=(kc == KC - 1)):
                return h.matmul(psB[0:1, 0:512], c.ones_col_bf[:, :], s[:, 0:512], start=st, stop=sp)
            pm1 = P.step("pe", f1, sem="pe")
            pm2 = P.step("pe", f2, sem="pe")
            pm_hist[kc] = (pm1, pm2)
        rs = c.rowscr[0:1, 0:512]
        rs2 = c.rowscr2[0:1, 0:512]
        rr = c.r_row[0:1, 0:512]
        nm = c.nmr_row[0:1, 0:512]
        P.want("act", pm2)
        a1 = P.step("act", lambda h, rs=rs: h.activation(rs, psA[0:1, 0:512], AF.Square, scale=1.0 / D), sem="act")
        P.want("dve", a1)
        d1 = P.step("dve", lambda h, rs=rs, rs2=rs2: h.scalar_tensor_tensor(rs2, psB[0:1, 0:512], 1.0 / D, rs, OP.mult, OP.subtract), sem="dve")
        P.want("act", d1)
        P.step("act", lambda h, rs2=rs2: h.activation(rs2, rs2, AF.Ln, bias=c.epsln[0:1, 0:1]), sem="act")
        a2 = P.step("act", lambda h, rr=rr, rs2=rs2: h.activation(rr, rs2, AF.Exp, scale=-0.5), sem="act")
        d2 = a2
        if half == 1:
            P.want("act", *stat_ms[0])
        a3 = P.step("act", lambda h, rs=rs: h.activation(rs, psA[0:1, 0:512], AF.Copy, scale=1.0 / D), sem="act")
        P.want("dve", a3, d2)
        P.step("dve", lambda h, rs=rs, rr=rr, nm=nm: h.tensor_tensor(nm, rs, rr, OP.mult), sem="dve")
        d4 = P.step("dve", lambda h, nm=nm: h.tensor_scalar(nm, nm, -1.0, None, OP.mult, OP.bypass), sem="dve")
        hold_pe = [a3, d1]
        P.want("pe", d2, d4)
        pcs = (psC, psD) if half == 0 else (psA, psB)
        if half == 1:
            P.want("pe", *hold_pe)
        b1 = P.step("pe", lambda h, pcs=pcs, rr=rr: h.matmul(pcs[0][:, 0:512], c.ones_row[:, :], rr, start=True, stop=True), sem="pe")
        b2 = P.step("pe", lambda h, pcs=pcs, nm=nm: h.matmul(pcs[1][:, 0:512], c.ones_row[:, :], nm, start=True, stop=True), sem="pe")
        stat_ms[half] = (b1, b2)
    b1, b2 = stat_ms[0]
    b3, b4 = stat_ms[1]
    bc = {0: (psC, psD), 1: (psA, psB)}
    sink_last = None
    for kc in range(KC):
        for half in range(2):
            co = half * 512
            psCh, psDh = bc[half]
            xr = c.XR[:, kc * TOK + co: kc * TOK + co + 512]
            P.want("dve", b1, b2, b3, b4)
            m1 = P.step("dve", lambda h, xr=xr, d=t_scr[kc % 2], psCh=psCh: h.tensor_tensor(d[:, 0:512], xr, psCh[:, 0:512], OP.mult), sem="dve")
            sink_last = xn_sink(kc, half, m1, t_scr[kc % 2], psDh)
    P.want("pe", sink_last)
    return sink_last



def emit_ffn(c, si):
    import contextlib
    P, nc, dr = c.P, c.nc, c.dr
    tag = f"f{si}_"
    with contextlib.ExitStack() as es:
        sb = lambda name, shape, dt: es.enter_context(nc.sbuf_tensor(tag + name, list(shape), dt))
        XNB = sb("XNB", (128, 16 * TOK), BF16)
        MIX = sb("MIX", (128, 16 * TOK), BF16)
        LG = sb("LG", (128, 8 * E), F32)
        M0TM = sb("M0TM", (128, 8 * E), F32)
        M1TM = sb("M1TM", (128, 8 * E), F32)
        SCR8 = sb("SCR8", (128, E), F32)
        JNK8 = sb("JNK8", (128, E), F32)
        Z0 = sb("Z0", (128, 8), F32)
        NZ0 = sb("NZ0", (128, 8), F32)
        Z1 = sb("Z1", (128, 8), F32)
        DEN = sb("DEN", (128, 8), F32)
        W0c = sb("W0c", (128, 8), F32)
        W1c = sb("W1c", (128, 8), F32)
        M0S = sb("M0S", (E, TOK), BF16)
        M1S = sb("M1S", (E, TOK), BF16)
        M0FB = sb("M0F", (128, TOK), BF16)
        M1FB = sb("M1F", (128, TOK), BF16)
        ROWSBF = sb("ROWSBF", (1, 3 * TOK), BF16)
        WB = sb("WB", (128, 2 * TOK), BF16)
        T1M = sb("T1M", (128, 2 * TOK), BF16)
        T2SB = sb("T2SB", (128, 2 * TOK), BF16)
        HSB = sb("HSB", (128, 1024), BF16)
        ASB = sb("ASB", (128, 1024), BF16)
        CASB = sb("CASB", (128, 1024), BF16)
        A1ST = sb("A1ST", (128, 16 * 128), BF16)
        B2ST = sb("B2ST", (128, D), BF16)
        XBF = [sb(f"XBF{i}", (128, 512), BF16) for i in range(2)]
        SQS = [sb(f"SQS{i}", (128, 512), BF16) for i in range(2)]
        _ts = sb("TS0", (128, 512), F32)
        TS = [_ts, _ts]

        PX = [c.PSF[:, 0:512], c.PSF[:, 512:1024]]
        PY = [c.PSF[:, 1024:1536], c.PSF[:, 1536:2048]]
        PT2 = c.PSG[:, :]

        P.full_barrier()
        c.psum_guard = []

        def xn_sink(kc, half, t_ms, t_src, psDh):
            dst = XNB[:, kc * TOK + half * 512: kc * TOK + half * 512 + 512]
            return P.step("dve", lambda h, d=dst, s=t_src, psDh=psDh: h.tensor_tensor(d, s[:, 0:512], psDh[:, 0:512], OP.add), sem="dve")
        if CUT < 1:
            c.psum_guard = P.barrier_ms()
            P.full_barrier()
            return
        emit_ln(c, PX[0], PX[1], PY[0], PY[1], xn_sink, SQS, XBF, TS)
        xn_done = P.barrier_ms()
        ln_psum_readers = list(xn_done)

        P.want("po", c.consts_ms, *c.stage_guard.get("wstC0") or [])
        P.step("po", lambda h: h.dma_start(out=c.wstC[0][:, 0:KC * E].rearrange("p (k e) -> p k e", k=KC), in_=dr["Wr"].rearrange("(k p) e -> p k e", p=128)), sem="wC0", amt=16)
        wr_ms = ("wC0", P.cnt["wC0"])
        P.step("po", lambda h: h.dma_start(out=A1ST[:, :].rearrange("p (k e) -> p k e", k=KC), in_=dr["A1s"].rearrange("(k p) e -> p k e", p=128)), sem="wmisc", amt=16)
        P.step("po", lambda h: h.dma_start(out=B2ST[:, :], in_=dr["B2s"]), sem="wmisc", amt=16)
        a1_ms = b2_ms = ("wmisc", P.cnt["wmisc"])

        lg_ms = []
        for t in range(8):
            P.want("pe", wr_ms, *(xn_done if t == 0 else []), *(ln_psum_readers if t == 0 else []))
            n = KC + 1
            for kc in range(KC):
                def fmm(h, kc=kc, t=t, st=(kc == 0)):
                    return h.matmul(PX[0][:, 0:E], XNB[:, kc * TOK + t * 128: kc * TOK + t * 128 + 128],
                                    c.wstC[0][:, kc * E: kc * E + E], start=st, stop=False)
                P.step("pe", fmm)
            pm = P.step("pe", lambda h: h.matmul(PX[0][:, 0:E], c.ones_row_bf[:, :], c.br_row[:, :], start=False, stop=True), sem="pe")
            P.want("act", pm)
            am = P.step("act", lambda h, t=t: h.activation(LG[:, t * E:(t + 1) * E], PX[0][:, 0:E], AF.Copy), sem="act")
            P.want("pe", am)
            lg_ms.append(am)
        if CUT < 2:
            c.psum_guard = P.barrier_ms()
            P.full_barrier()
            return
        c.stage_guard["wstC0"] = [("pe", P.cnt["pe"])]
        dlast = None
        for t in range(8):
            sl = LG[:, t * E:(t + 1) * E]
            P.want("dve", lg_ms[t])
            P.step("dve", lambda h, sl=sl, t=t: h.reduce_max(Z0[:, t:t + 1], sl, axis=AX.X), sem="dve")
            P.step("dve", lambda h, t=t: h.tensor_scalar(NZ0[:, t:t + 1], Z0[:, t:t + 1], -1.0, None, OP.mult, OP.bypass), sem="dve")
            P.step("dve", lambda h, sl=sl, t=t: h.tensor_scalar(M0TM[:, t * E:(t + 1) * E], sl, Z0[:, t:t + 1], None, OP.is_equal, OP.bypass), sem="dve")
            P.step("dve", lambda h, sl=sl, t=t: h.scalar_tensor_tensor(SCR8[:, 0:E], M0TM[:, t * E:(t + 1) * E], NEG, sl, OP.mult, OP.add), sem="dve")
            P.step("dve", lambda h, t=t: h.reduce_max(Z1[:, t:t + 1], SCR8[:, 0:E], axis=AX.X), sem="dve")
            P.step("dve", lambda h, t=t: h.tensor_scalar(M1TM[:, t * E:(t + 1) * E], SCR8[:, 0:E], Z1[:, t:t + 1], None, OP.is_equal, OP.bypass), sem="dve")
            dlast = P.step("dve", lambda h, t=t: h.tensor_tensor(W1c[:, t:t + 1], Z1[:, t:t + 1], NZ0[:, t:t + 1], OP.add), sem="dve")
        a1 = None
        for t in range(8):
            sl = LG[:, t * E:(t + 1) * E]
            if t == 0:
                P.want("act", dlast)
            a1 = P.step("act", lambda h, sl=sl, t=t: h.activation(JNK8[:, 0:E], sl, AF.Exp, bias=NZ0[:, t:t + 1], accum_out=DEN[:, t:t + 1]), sem="act")
        a2 = P.step("act", lambda h: h.activation(W1c[:, 0:8], W1c[:, 0:8], AF.Exp), sem="act")
        P.want("dve", a1, a2)
        P.step("dve", lambda h: h.reciprocal(W0c[:, 0:8], DEN[:, 0:8]), sem="dve")
        d9 = P.step("dve", lambda h: h.tensor_tensor(W1c[:, 0:8], W1c[:, 0:8], W0c[:, 0:8], OP.mult), sem="dve")
        row_src = [d9] * 8
        if CUT < 2.7:
            c.psum_guard = P.barrier_ms()
            P.full_barrier()
            return
        last_row_act = None
        for name, col, off in (("w0", W0c, 0), ("w1", W1c, TOK)):
            P.want("pe", row_src[0], *( [last_row_act] if last_row_act else [] ))
            pm = None
            for t in range(8):
                pm = P.step("pe", lambda h, col=col, t=t: h.matmul(PT2[0:1, t * 128:(t + 1) * 128], col[:, t:t + 1], c.ident_f[:, :], start=True, stop=True), sem="pe" if t == 7 else None)
            P.want("act", pm)
            last_row_act = P.step("act", lambda h, off=off: h.activation(ROWSBF[0:1, off: off + TOK], PT2[0:1, 0:TOK], AF.Copy), sem="act")
            P.want("pe", last_row_act)
        if CUT < 3:
            c.psum_guard = P.barrier_ms()
            P.full_barrier()
            return
        P.want("dve", last_row_act)
        rbf = P.step("dve", lambda h: h.tensor_tensor(ROWSBF[0:1, 2 * TOK:3 * TOK], ROWSBF[0:1, 0:TOK], ROWSBF[0:1, TOK:2 * TOK], OP.add), sem="dve")
        P.want("pe", rbf)
        pm = None
        for j in range(4):
            pm = P.step("pe", lambda h, j=j: h.matmul(PT2[:, j * 512:(j + 1) * 512], c.ones_row_bf[:, :], ROWSBF[0:1, j * 512:(j + 1) * 512], start=True, stop=True), sem="pe" if j == 3 else None)
        P.want("act", pm)
        am = P.step("act", lambda h: h.activation(WB[:, :], PT2[:, :], AF.Copy), sem="act")
        P.want("pe", am)
        wb_ms = [am] * 4

        mask_ms = []
        for k, (MTM, MS, MF) in enumerate(((M0TM, M0S, M0FB), (M1TM, M1S, M1FB))):
            P.want("pe", row_src[0])
            pm = None
            for t in range(8):
                pm = P.step("pe", lambda h, MTM=MTM, t=t: h.transpose(PT2[0:E, t * 128:(t + 1) * 128], MTM[:, t * E:(t + 1) * E], c.ident_f[:, :]), sem="pe" if t == 7 else None)
            P.want("act", pm)
            am = P.step("act", lambda h, MS=MS: h.activation(MS[0:E, 0:TOK], PT2[0:E, 0:TOK], AF.Copy), sem="act")
            tr_ms = [am]
            for half in range(2):
                P.want("pe", *tr_ms)
                pm = P.step("pe", lambda h, MS=MS, half=half: h.matmul(PX[1][:, 0:512], c.E16[:, :], MS[0:E, half * 512: half * 512 + 512], start=True, stop=True), sem="pe")
                P.want("dve", pm)
                dm = P.step("dve", lambda h, MF=MF, half=half: h.tensor_scalar(MF[:, half * 512: half * 512 + 512], PX[1][:, 0:512], 2.0, None, OP.mult, OP.bypass), sem="dve")
                P.want("pe", dm)
                mask_ms.append(dm)
        if CUT < 4:
            c.psum_guard = P.barrier_ms()
            P.full_barrier()
            return
        pt2_free = [("act", P.cnt["act"]), ("pe", P.cnt["pe"])]

        t1_ms = []
        for half in range(2):
            P.want("pe", a1_ms, *pt2_free)
            for kc in range(KC):
                def fmm(h, kc=kc, half=half, st=(kc == 0), sp=(kc == KC - 1)):
                    return h.matmul(PY[0][:, 0:512], A1ST[:, kc * 128:(kc + 1) * 128],
                                    XNB[:, kc * TOK + half * 512: kc * TOK + half * 512 + 512], start=st, stop=sp)
                pm = P.step("pe", fmm, sem="pe" if kc == KC - 1 else None)
            for k, MF in enumerate((M0FB, M1FB)):
                P.want("dve", pm, mask_ms[2 * k + half])
                dm = P.step("dve", lambda h, k=k, half=half, MF=MF: h.tensor_tensor(
                    T1M[:, k * TOK + half * 512: k * TOK + half * 512 + 512], PY[0][:, 0:512],
                    MF[:, half * 512: half * 512 + 512], OP.mult), sem="dve")
                t1_ms.append(dm)
                P.want("pe", dm)

        if CUT < 5:
            c.psum_guard = P.barrier_ms()
            P.full_barrier()
            return
        B2R = A1ST[0:1, 0:D]
        P.want("sp", *(c.stage_guard.get("b2r") or []), ("pe", P.cnt["pe"]))
        P.step("sp", lambda h: h.dma_start(out=B2R, in_=dr["b2_row"]), sem="cst", amt=16)
        b2r_ms = ("cst", P.cnt["cst"])
        def stage_f(f):
            buf = f % 2
            P.want("po", *c.stage_guard.get(f"wstA{buf}") or [])
            m1 = P.step("po", lambda h, f=f, buf=buf: h.dma_start(
                out=c.wstA[buf][:, :].rearrange("p (k q) -> p k q", k=KC),
                in_=dr["W1"][:, f * 128:(f + 1) * 128].rearrange("(k p) q -> p k q", p=128)), sem=f"wA{buf}", amt=16)
            P.want("po", *c.stage_guard.get(f"wstC{buf}") or [])
            P.step("po", lambda h, f=f, buf=buf: h.dma_start(
                out=c.wstC[buf][:, 0:128], in_=dr["B1s"][:, f * 128:(f + 1) * 128]), sem=f"wC{buf}", amt=16)
            P.step("po", lambda h, f=f, buf=buf: h.dma_start(
                out=c.wstC[buf][:, 128:256], in_=dr["A2s"][f * 128:(f + 1) * 128, :]), sem=f"wC{buf}", amt=16)
            m3 = (f"wC{buf}", P.cnt[f"wC{buf}"])
            return [m1, m3]

        def stage_w2(sup, d, buf):
            P.want("po", *c.stage_guard.get(f"wstB{buf}") or [])
            return P.step("po", lambda h, sup=sup, d=d, buf=buf: h.dma_start(
                out=c.wstB[buf][:, :].rearrange("p (q dd) -> p q dd", q=KC),
                in_=dr["W2"][sup * 2048:(sup + 1) * 2048, d * 128:(d + 1) * 128].rearrange("(q p) dd -> p q dd", p=128)), sem=f"wB{buf}", amt=16)

        stage_ms = {0: stage_f(0)}
        ypend = {"fn": None}
        hsb2_rd = {0: [], 1: []}
        slot_t = {0: PX[0], 1: PX[1], 2: PY[0], 3: PY[1]}
        x_free = {0: [], 1: []}
        y_free = {0: [], 1: []}
        pd_free = {}
        hsb_rd = []
        asb_rd = {0: [], 1: []}
        casb_rd = {0: [], 1: []}
        w2_ms = {}
        mixw_guard = []
        for sup in range(4):
            first_mix = True
            for fi in range(16):
                f = sup * 16 + fi
                buf = f % 2
                for half in range(2):
                    X = PX[half]
                    P.want("pe", *stage_ms[f], *x_free[half])
                    for kc in range(KC):
                        def fmm(h, kc=kc, half=half, buf=buf, X=X, st=(kc == 0), sp=(kc == KC - 1)):
                            return h.matmul(X[:, 0:512], c.wstA[buf][:, kc * 128:(kc + 1) * 128],
                                            XNB[:, kc * TOK + half * 512: kc * TOK + half * 512 + 512], start=st, stop=sp)
                        pmx = P.step("pe", fmm, sem="pe" if kc == KC - 1 else None)
                    P.want("act", pmx, *hsb2_rd[half])
                    hm = P.step("act", lambda h, X=X, f=f, half=half: h.activation(
                        HSB[:, half * 512: half * 512 + 512], X[:, 0:512], AF.Identity,
                        bias=c.b1_col[:, f:f + 1]), sem="act")
                    x_free[half] = [hm]

                    def mk_yphase(f=f, fi=fi, half=half, buf=buf, hm=hm):
                        def emit_y():
                            hsb2_rd[half] = []
                            for k in range(2):
                                Y = PY[k]
                                P.want("pe", *y_free[k], t1_ms[2 * k + half], hm)
                                def fl1(h, k=k, half=half, buf=buf, Y=Y):
                                    return h.matmul(Y[:, 0:512], c.wstC[buf][:, 0:128],
                                                    T1M[:, k * TOK + half * 512: k * TOK + half * 512 + 512], start=True, stop=False)
                                P.step("pe", fl1)
                                def fcp(h, Y=Y, half=half):
                                    return h.matmul(Y[:, 0:512], c.ident[:, :], HSB[:, half * 512: half * 512 + 512], start=False, stop=True)
                                pmy = P.step("pe", fcp, sem="pe")
                                hsb2_rd[half].append(pmy)
                                P.want("act", pmy, *asb_rd[k])
                                gm = P.step("act", lambda h, k=k, Y=Y: h.activation(ASB[:, k * 512:(k + 1) * 512], Y[:, 0:512], AF.Gelu_apprx_tanh), sem="act")
                                y_free[k] = [gm]
                                P.want("dve", gm, wb_ms[2 * k + half], *casb_rd[k])
                                cm = P.step("dve", lambda h, k=k, half=half: h.tensor_tensor(
                                    CASB[:, k * 512:(k + 1) * 512], ASB[:, k * 512:(k + 1) * 512],
                                    WB[:, k * TOK + half * 512: k * TOK + half * 512 + 512], OP.mult), sem="dve")
                                asb_rd[k] = [cm]
                                P.want("pe", cm, stage_ms[f][1])
                                def ft2(h, k=k, half=half, buf=buf, st=(f == 0), sp=(f == 63)):
                                    sl = PT2[:, (2 * k + half) * 512: (2 * k + half) * 512 + 512]
                                    return h.matmul(sl, c.wstC[buf][:, 128:256], CASB[:, k * 512:(k + 1) * 512], start=st, stop=sp)
                                t2m_pe = P.step("pe", ft2, sem="pe")
                                casb_rd[k] = [t2m_pe]
                            c.stage_guard[f"wstC{buf}"] = [("pe", P.cnt["pe"])]
                            mmx = P.step("dve", lambda h, fi=fi, half=half: h.tensor_tensor(
                                MIX[:, fi * TOK + half * 512: fi * TOK + half * 512 + 512],
                                CASB[:, 0:512], CASB[:, 512:1024], OP.add), sem="dve")
                            casb_rd[0].append(mmx)
                            casb_rd[1].append(mmx)
                            return t2m_pe, mmx
                        return emit_y
                    if first_mix:
                        P.want("dve", *mixw_guard)
                        first_mix = False
                    if ypend["fn"] is not None:
                        t2m_pe, mm = ypend["fn"]()
                    ypend["fn"] = mk_yphase()
                    if half == 1 and f + 1 < 64:
                        stage_ms[f + 1] = stage_f(f + 1)
                c.stage_guard[f"wstA{buf}"] = [("pe", P.cnt["pe"])]
            if ypend["fn"] is not None:
                t2m_pe, mm = ypend["fn"]()
                ypend["fn"] = None
            mix_done = mm
            if sup == 3:
                t2sb_ms = []
                for k in range(2):
                    for half in range(2):
                        P.want("dve", t2m_pe, mask_ms[2 * k + half])
                        dm = P.step("dve", lambda h, k=k, half=half, MF=(M0FB if k == 0 else M1FB): h.tensor_tensor(
                            T2SB[:, k * TOK + half * 512: k * TOK + half * 512 + 512],
                            PT2[:, (2 * k + half) * 512:(2 * k + half) * 512 + 512],
                            MF[:, half * 512:half * 512 + 512], OP.mult), sem="dve")
                        t2sb_ms.append(dm)
            pd_free = {0: x_free[0], 1: x_free[1], 2: y_free[0], 3: y_free[1]}
            w2_ms[(sup, 0)] = stage_w2(sup, 0, 0)
            for d in range(16):
                if d + 1 < 16:
                    w2_ms[(sup, d + 1)] = stage_w2(sup, d + 1, (d + 1) % 2)
                wbuf = d % 2
                for half in range(2):
                    slot = (2 * d + half) % 4
                    Pd = slot_t[slot]
                    P.want("pe", w2_ms[(sup, d)], mix_done, *pd_free[slot])
                    last_plain = (sup != 3)
                    for fi in range(KC):
                        sp = last_plain and (fi == KC - 1)
                        def fw2(h, fi=fi, half=half, wbuf=wbuf, Pd=Pd, st=(fi == 0), sp=sp):
                            return h.matmul(Pd[:, 0:512], c.wstB[wbuf][:, fi * 128:(fi + 1) * 128],
                                            MIX[:, fi * TOK + half * 512: fi * TOK + half * 512 + 512],
                                            start=st, stop=sp)
                        pm = P.step("pe", fw2, sem="pe" if sp else None)
                    if sup == 3:
                        for k in range(2):
                            P.want("pe", t2sb_ms[2 * k + half], b2_ms, b2r_ms)
                            def fl2(h, k=k, half=half, d=d, Pd=Pd):
                                return h.matmul(Pd[:, 0:512], B2ST[:, d * 128:(d + 1) * 128],
                                                T2SB[:, k * TOK + half * 512: k * TOK + half * 512 + 512],
                                                start=False, stop=False)
                            P.step("pe", fl2)
                        def fb2(h, half=half, d=d, Pd=Pd):
                            return h.matmul(Pd[:, 0:512], B2R[0:1, d * 128:(d + 1) * 128],
                                            ROWSBF[0:1, 2 * TOK + half * 512: 2 * TOK + half * 512 + 512],
                                            start=False, stop=True)
                        pm = P.step("pe", fb2, sem="pe")
                    P.want("dve", pm)
                    xr_sl = c.XR[:, d * TOK + half * 512: d * TOK + half * 512 + 512]
                    em = P.step("dve", lambda h, Pd=Pd, xr_sl=xr_sl: h.tensor_tensor(xr_sl, Pd[:, 0:512], xr_sl, OP.add), sem="dve")
                    pd_free[slot] = [em]
                    if sup == 3 and half == 1:
                        c.xr_chunk[d] = [em]
                c.stage_guard[f"wstB{wbuf}"] = [("pe", P.cnt["pe"])]
            x_free = {0: pd_free[0], 1: pd_free[1]}
            y_free = {0: pd_free[2], 1: pd_free[3]}
            mixw_guard = [("pe", P.cnt["pe"])]
        c.stage_guard["b2r"] = [("pe", P.cnt["pe"])]
        c.xr_ready = [("dve", P.cnt["dve"])]
        c.xr_dirty = True
        c.psum_guard = [("dve", P.cnt["dve"]), ("act", P.cnt["act"])]
        P.full_barrier()


PAIRS = [[0, 1], [2, 3], [4, 5], [6, 7]]


def emit_attn_pre(c, si, kind):
    import contextlib
    P, nc, dr = c.P, c.nc, c.dr
    tag = f"p{si}_"
    P.full_barrier()
    c.pre_body_pe = ("pe", P.cnt["pe"])
    if not hasattr(c, "xng_free"):
        c.xng_free = {g: [] for g in range(4)}
        c.xngo_free = {g: [] for g in range(4)}
    c.gather_ms = {}
    c.gather_want = {}
    if c.xr_dirty:
        emit_spill_x(c)
    with contextlib.ExitStack() as es:
        sb = lambda n, s, d: es.enter_context(nc.sbuf_tensor(tag + n, list(s), d))
        XBF = [sb(f"XBF{i}", (128, 512), BF16) for i in range(2)]
        SQS = [sb(f"SQS{i}", (128, 512), BF16) for i in range(2)]
        _ts = sb("TS", (128, 512), F32)
        TS = [_ts, _ts]
        XNS = [sb(f"XNS{i}", (128, 512), BF16) for i in range(4)]
        psA, psB = c.PSF[:, 0:512], c.PSF[:, 512:1024]
        psC, psD = c.PSF[:, 1024:1536], c.PSF[:, 1536:2048]
        P.want("pe", *c.psum_guard)
        c.psum_guard = []
        nstage = [0]
        xns_last = {}

        def xn_sink(kc, half, t_ms, t_src, psDh):
            buf = (kc % 2) * 2 + half
            if buf in xns_last:
                P.want("dve", xns_last[buf])
            dm = P.step("dve", lambda h, d=XNS[buf], s=t_src, psDh=psDh: h.tensor_tensor(
                d[:, 0:512], s[:, 0:512], psDh[:, 0:512], OP.add), sem="dve")
            g = kc // 4
            P.want("sp", dm, *(c.xng_free[g] if (kc % 4 == 0 and half == 0) else []))
            ms = P.step("sp", lambda h, kc=kc, half=half, buf=buf, g=g: h.dma_start(
                out=c.xn_g[g][(kc % 4) * 128:(kc % 4 + 1) * 128, half * 512: half * 512 + 512],
                in_=XNS[buf][:, 0:512]), sem=f"xg{buf}", amt=16)
            xns_last[buf] = ms
            nstage[0] += 1
            if kc % 4 == 3 and half == 1:
                c.gather_want[g] = [(f"xg{b}", P.cnt[f"xg{b}"]) for b in range(4)]
            return dm

        emit_ln(c, psA, psB, psC, psD, xn_sink, SQS, XBF, TS)
        c.psum_guard = [("dve", P.cnt["dve"]), ("act", P.cnt["act"])]
        if not c.xr_dirty:
            c.xnf_region_guard = [("dve", P.cnt["dve"])]


def emit_attn_body(c, si, kind):
    import contextlib
    P, nc, dr = c.P, c.nc, c.dr
    tag = f"b{si}_"
    deltanet = (kind == "attn_b")
    pre = "b" if deltanet else "a"
    with contextlib.ExitStack() as es:
        sb = lambda n, s, d: es.enter_context(nc.sbuf_tensor(tag + n, list(s), d))
        XNF = c.XNF
        KG = sb("KG", (128, 4 * 2048), BF16)
        STEXP = sb("STEXP", (128, 16 * 512), BF16)
        SH = STEXP[:, 0:2048]
        RT = STEXP[:, 2048:4096]
        COSF = STEXP[:, 4096:6144]
        SINF = STEXP[:, 6144:8192]
        VG = sb("VG", (128, 16 * 512), BF16)
        QG = KG if deltanet else sb("QG", (128, 4 * 2048), BF16)
        KBG = sb("KBG", (128, 4 * 2048), BF16) if deltanet else None


        _wst = sb("WST0", (128, 16 * 512), BF16)
        WST = [_wst, _wst]

        wfirst = pre + ("Wk" if deltanet else "Wq")
        pre_widx = 1 if deltanet else 0
        pre_sem = f"wA{pre_widx % 2}"
        P.want("po", c.pre_body_pe,
               *(c.stage_guard.get("bodyW0") or []), *(c.stage_guard.get("bodyW1") or []))
        P.step("po", lambda h: h.dma_start(
            out=_wst[:, :].rearrange("p (k q) -> p k q", k=KC),
            in_=dr[wfirst][:, 0:4 * 128].rearrange("(k p) q -> p k q", p=128)), sem=pre_sem, amt=16)
        prestage_ms = (pre_sem, P.cnt[pre_sem])

        for g in range(4):
            P.want("po", *c.gather_want[g], *c.xngo_free[g])
            cm = P.step("po", lambda h, g=g: h.collective_compute(
                "AllGather", OP.bypass, replica_groups=PAIRS,
                ins=[c.xn_g[g].ap().opt()], outs=[c.xn_go[g].ap().opt()]), sem=f"cc{g}", amt=1)
            c.gather_ms[g] = cm
            c.xng_free[g] = [cm]

        c.ctx_cc_g = {}
        if not hasattr(c, "xb_free"):
            c.xb_free = {}
        xb_ms = {}
        for g in range(4):
            P.want("sp", c.gather_ms[g], *getattr(c, "xnf_region_guard", []), *c.xb_free.get(g, []))
            for ci in range(4):
                kc = g * 4 + ci
                P.step("sp", lambda h, g=g, ci=ci, kc=kc: h.dma_start(
                    out=XNF[:, kc * 2048: kc * 2048 + 1024],
                    in_=c.xn_go[g][ci * 128:(ci + 1) * 128, :]), sem=f"xb{g}", amt=16)
                P.step("sp", lambda h, g=g, ci=ci, kc=kc: h.dma_start(
                    out=XNF[:, kc * 2048 + 1024: kc * 2048 + 2048],
                    in_=c.xn_go[g][512 + ci * 128: 512 + ci * 128 + 128, :]), sem=f"xb{g}", amt=16)
            xb_ms[g] = (f"xb{g}", P.cnt[f"xb{g}"])

        ctx_dma_ms = []
        attn_done_per_group = []
        BETST = sb("BETST", (128, 64), BF16) if deltanet else None
        CTXF = sb("CTXF", (128, 2048), BF16)
        KEEP = sb("KEEP", (128, 2048), BF16)
        BCS = sb("BCS", (128, 512), BF16)
        RECB = sb("RECB", (1, 512), BF16)
        RSROW = CTXF[0:1, 0:2048]
        BETA = CTXF[32:36, 0:2048] if deltanet else None
        for G in range(2):
            P.full_barrier()
            P.want("sp", ("pe", P.cnt["pe"]), ("act", P.cnt["act"]), ("dve", P.cnt["dve"]))
            P.step("sp", lambda h: h.dma_start(out=COSF, in_=dr["cosf"]), sem="cst", amt=16)
            P.step("sp", lambda h: h.dma_start(out=SINF, in_=dr["sinf"]), sem="cst", amt=16)
            cst1 = cst2 = ("cst", P.cnt["cst"])
            P.want("dve", ("pe", P.cnt["pe"]), cst1)
            if True:
                QA = c.PSF[:, :]
                QB = c.PSG[:, :]
                P.want("pe", *c.psum_guard)
                c.psum_guard = []

                def stage(widx, wname, cols, slot_guard):
                    sem = f"wA{widx % 2}"
                    P.want("po", *slotg[0], *slotg[1])
                    P.step("po", lambda h, wname=wname, cols=cols, widx=widx: h.dma_start(
                        out=WST[widx % 2][:, :].rearrange("p (k q) -> p k q", k=KC),
                        in_=dr[wname][:, cols[0]:cols[1]].rearrange("(k p) q -> p k q", p=128)), sem=sem, amt=16)
                    return (sem, P.cnt[sem])

                cols = (G * 4 * 128, (G + 1) * 4 * 128)
                psg = {0: list(c.psum_guard), 1: list(c.psum_guard)}
                c.psum_guard = []
                slotg = {0: list(c.stage_guard.get("bodyW0") or []),
                         1: list(c.stage_guard.get("bodyW1") or [])}

                def produce_fm(bufT, widx, wms):
                    evs = []
                    for hh in range(4):
                        pi = hh % 2
                        PSU = QA if pi == 0 else QB
                        P.want("pe", wms, *psg[pi])
                        pm = None
                        for kc in range(KC):
                            if kc % 4 == 0:
                                P.want("pe", xb_ms[kc // 4])
                            for n in range(4):
                                def fmm(h, hh=hh, kc=kc, n=n, PSU=PSU, widx=widx,
                                        st=(kc == 0), sp=(kc == KC - 1)):
                                    return h.matmul(PSU[:, n * 512:(n + 1) * 512],
                                                    WST[widx % 2][:, kc * 512 + hh * 128: kc * 512 + (hh + 1) * 128],
                                                    XNF[:, kc * 2048 + n * 512: kc * 2048 + (n + 1) * 512],
                                                    start=st, stop=sp)
                                pm = P.step("pe", fmm, sem="pe" if (kc == KC - 1 and n == 3) else None)
                        P.want("act", pm)
                        am = P.step("act", lambda h, hh=hh, PSU=PSU, bufT=bufT: h.activation(
                            bufT[:, hh * 2048:(hh + 1) * 2048], PSU[:, :], AF.Copy), sem="act")
                        psg[pi] = [am]
                        evs.append(am)
                    slotg[widx % 2] = [("pe", P.cnt["pe"])]
                    return evs

                def produce_v(wms):
                    evs = []
                    for t4 in range(4):
                        pi = t4 % 2
                        PSU = QA if pi == 0 else QB
                        P.want("pe", wms, *psg[pi])
                        pm = None
                        for ti in range(4):
                            tile = t4 * 4 + ti
                            for kc in range(KC):
                                if kc % 4 == 0:
                                    P.want("pe", xb_ms[kc // 4])
                                def fmv(h, tile=tile, ti=ti, kc=kc, PSU=PSU, st=(kc == 0)):
                                    return h.matmul(PSU[:, ti * 512:(ti + 1) * 512],
                                                    XNF[:, kc * 2048 + tile * 128: kc * 2048 + (tile + 1) * 128],
                                                    WST[0][:, kc * 512:(kc + 1) * 512],
                                                    start=st, stop=(kc == KC - 1))
                                pm = P.step("pe", fmv, sem="pe" if (kc == KC - 1 and ti == 3) else None)
                        P.want("act", pm)
                        am = P.step("act", lambda h, t4=t4, PSU=PSU: h.activation(
                            VG[:, t4 * 2048:(t4 + 1) * 2048], PSU[:, :], AF.Copy), sem="act")
                        psg[pi] = [am]
                        evs.append(am)
                    slotg[0] = [("pe", P.cnt["pe"])]
                    return evs

                if not deltanet:
                    ms_q = prestage_ms if G == 0 else stage(0, pre + "Wq", cols, None)
                    q_ev = produce_fm(QG, 0, ms_q)
                    ms_k = stage(1, pre + "Wk", cols, None)
                    k_ev = produce_fm(KG, 1, ms_k)
                    ms_v = stage(0, pre + "Wv", cols, None)
                    v_ev = produce_v(ms_v)
                else:
                    ms_k = prestage_ms if G == 0 else stage(1, pre + "Wk", cols, None)
                    k_ev = produce_fm(KG, 1, ms_k)
                    ms_v = stage(0, pre + "Wv", cols, None)
                    v_ev = produce_v(ms_v)

                if deltanet:
                    P.want("po", *c.stage_guard.get("bodyBET") or [])
                    bms = P.step("po", lambda h, G=G, BETST=BETST: h.dma_start(
                        out=BETST[:, :].rearrange("p (k q) -> p k q", k=KC),
                        in_=dr["bWb"][:, G * 4:(G + 1) * 4].rearrange("(k p) q -> p k q", p=128)), sem="wmisc", amt=16)
                    P.want("pe", bms, *v_ev)
                    for kc in range(KC):
                        if kc % 4 == 0:
                            P.want("pe", xb_ms[kc // 4])
                        for n in range(4):
                            def fmb(h, kc=kc, n=n, st=(kc == 0)):
                                return h.matmul(QA[0:4, n * 512:(n + 1) * 512],
                                                BETST[:, kc * 4:(kc + 1) * 4],
                                                XNF[:, kc * 2048 + n * 512: kc * 2048 + (n + 1) * 512],
                                                start=st, stop=(kc == KC - 1))
                            pm = P.step("pe", fmb, sem="pe" if (kc == KC - 1 and n == 3) else None)
                    P.want("act", pm)
                    bsig = P.step("act", lambda h, BETA=BETA: h.activation(BETA[0:4, :], QA[0:4, :], AF.Sigmoid, scale=1.0), sem="act")
                    P.want("pe", bsig)
                    c.stage_guard["bodyBET"] = [("pe", P.cnt["pe"])]

                pairmask = []
                for i in range(16):
                    pairmask += [2 * i + 1, 2 * i]
                rope_done = []
                for hh in range(4):
                    kh = KG[:, hh * 2048:(hh + 1) * 2048]
                    if deltanet:
                        P.want("dve", k_ev[hh])
                        d1 = P.step("dve", lambda h, kh=kh: h.tensor_tensor(SH[:, :], kh, kh, OP.mult), sem="dve")
                        P.want("pe", d1)
                        for n in range(4):
                            pm = P.step("pe", lambda h, n=n, st=True: h.matmul(
                                QB[0:1, n * 512:(n + 1) * 512], c.ones_col_bf[:, :],
                                SH[:, n * 512:(n + 1) * 512], start=True, stop=True),
                                sem="pe" if n == 3 else None)
                        P.want("act", pm)
                        P.step("act", lambda h: h.activation(RSROW[0:1, :], QB[0:1, :], AF.Ln, bias=c.epsl2[0:1, 0:1]), sem="act")
                        d2 = P.step("act", lambda h: h.activation(RSROW[0:1, :], RSROW[0:1, :], AF.Exp, scale=-0.5), sem="act")
                        P.want("pe", d2)
                        for n in range(4):
                            pm = P.step("pe", lambda h, n=n: h.matmul(
                                QA[:, n * 512:(n + 1) * 512], c.ones_row_bf[:, :],
                                RSROW[0:1, n * 512:(n + 1) * 512], start=True, stop=True),
                                sem="pe" if n == 3 else None)
                        P.want("dve", pm)
                        d3 = P.step("dve", lambda h, kh=kh: h.tensor_tensor(kh, kh, QA[:, :], OP.mult), sem="dve")
                        P.want("pe", d3)
                    P.want("dve", k_ev[hh], cst1, cst2)
                    r1 = P.step("dve", lambda h, kh=kh: h.stream_shuffle(SH[:, :], kh, pairmask), sem="dve")
                    r2 = P.step("dve", lambda h, kh=kh: h.tensor_tensor(RT[:, :], kh, COSF[:, :], OP.mult), sem="dve")
                    r3 = P.step("dve", lambda h: h.tensor_tensor(SH[:, :], SH[:, :], SINF[:, :], OP.mult), sem="dve")
                    r4 = P.step("dve", lambda h, kh=kh: h.tensor_tensor(kh, RT[:, :], SH[:, :], OP.add), sem="dve")
                    rope_done.append(r4)
                    if deltanet:
                        P.want("pe", r4)
                        for n in range(4):
                            pm = P.step("pe", lambda h, hh=hh, n=n, BETA=BETA: h.matmul(
                                QB[0:1, n * 512:(n + 1) * 512], c.oh32[32:36, hh:hh + 1],
                                BETA[0:4, n * 512:(n + 1) * 512], start=True, stop=True),
                                sem="pe" if n == 3 else None)
                        P.want("act", pm)
                        a2 = P.step("act", lambda h: h.activation(RSROW[0:1, :], QB[0:1, :], AF.Copy, scale=2.0), sem="act")
                        P.want("pe", a2)
                        for n in range(4):
                            pm = P.step("pe", lambda h, n=n: h.matmul(
                                QA[:, n * 512:(n + 1) * 512], c.ones_row_bf[:, :],
                                RSROW[0:1, n * 512:(n + 1) * 512], start=True, stop=True),
                                sem="pe" if n == 3 else None)
                        P.want("dve", pm)
                        d5 = P.step("dve", lambda h, hh=hh, kh=kh: h.tensor_tensor(
                            KBG[:, hh * 2048:(hh + 1) * 2048], kh, QA[:, :], OP.mult), sem="dve")
                        P.want("pe", d5)
                if not deltanet:
                    for hh in range(4):
                        qh = QG[:, hh * 2048:(hh + 1) * 2048]
                        P.want("dve", q_ev[hh])
                        P.step("dve", lambda h, qh=qh: h.stream_shuffle(SH[:, :], qh, pairmask), sem="dve")
                        P.step("dve", lambda h, qh=qh: h.tensor_tensor(RT[:, :], qh, COSF[:, :], OP.mult), sem="dve")
                        P.step("dve", lambda h: h.tensor_tensor(SH[:, :], SH[:, :], SINF[:, :], OP.mult), sem="dve")
                        P.step("dve", lambda h, qh=qh: h.tensor_tensor(qh, RT[:, :], SH[:, :], OP.add), sem="dve")
                c.psum_guard = [("dve", P.cnt["dve"]), ("act", P.cnt["act"]), ("pe", P.cnt["pe"])]
                c.stage_guard["bodyW0"] = [("pe", P.cnt["pe"])]
                c.stage_guard["bodyW1"] = [("pe", P.cnt["pe"])]

            P.full_barrier()
            if deltanet:
                emit_scan(c, tag, G, KG, KBG, VG)
                P.full_barrier()

            if deltanet:
                if True:
                    WSQ = WST[0]
                    QA = c.PSF[:, :]
                    QB = c.PSG[:, :]
                    P.want("pe", *c.psum_guard)
                    c.psum_guard = []
                    P.want("po", *(c.stage_guard.get("bodyWSQ") or []), *(c.stage_guard.get("bodyW0") or []))
                    wq_ms = P.step("po", lambda h, G=G, WSQ=WSQ: h.dma_start(
                        out=WSQ[:, :].rearrange("p (k q) -> p k q", k=KC),
                        in_=dr["bWq"][:, G * 512:(G + 1) * 512].rearrange("(k p) q -> p k q", p=128)), sem="wmisc", amt=16)
                    qev = []
                    for hh in range(4):
                        PSU = QA if hh % 2 == 0 else QB
                        P.want("pe", wq_ms, *(qev[hh - 2:hh - 1] if hh >= 2 else []))
                        for kc in range(KC):
                            if kc % 4 == 0:
                                P.want("pe", xb_ms[kc // 4])
                            for n in range(4):
                                def fmq(h, hh=hh, kc=kc, n=n, PSU=PSU, st=(kc == 0)):
                                    return h.matmul(PSU[:, n * 512:(n + 1) * 512],
                                                    WSQ[:, kc * 512 + hh * 128: kc * 512 + (hh + 1) * 128],
                                                    XNF[:, kc * 2048 + n * 512: kc * 2048 + (n + 1) * 512],
                                                    start=st, stop=(kc == KC - 1))
                                pm = P.step("pe", fmq, sem="pe" if (kc == KC - 1 and n == 3) else None)
                        P.want("act", pm)
                        am = P.step("act", lambda h, hh=hh, PSU=PSU: h.activation(
                            KG[:, hh * 2048:(hh + 1) * 2048], PSU[:, :], AF.Copy), sem="act")
                        qev.append(am)
                    pairmask = []
                    for i in range(16):
                        pairmask += [2 * i + 1, 2 * i]
                    for hh in range(4):
                        qh = KG[:, hh * 2048:(hh + 1) * 2048]
                        P.want("dve", qev[hh])
                        P.step("dve", lambda h, qh=qh: h.stream_shuffle(SH[:, :], qh, pairmask), sem="dve")
                        P.step("dve", lambda h, qh=qh: h.tensor_tensor(RT[:, :], qh, COSF[:, :], OP.mult), sem="dve")
                        P.step("dve", lambda h: h.tensor_tensor(SH[:, :], SH[:, :], SINF[:, :], OP.mult), sem="dve")
                        P.step("dve", lambda h, qh=qh: h.tensor_tensor(qh, RT[:, :], SH[:, :], OP.add), sem="dve")
                    c.psum_guard = [("dve", P.cnt["dve"]), ("act", P.cnt["act"])]
                    c.stage_guard["bodyWSQ"] = [("pe", P.cnt["pe"])]
                P.full_barrier()

            if DEV_TAPS and G == 0:
                P.full_barrier()
                for tp_ap, buf in ((c.tapA, VG), (c.tapB, KBG if deltanet else KG)):
                    with nc.sbuf_tensor(tag + f"tp{id(tp_ap) % 97}", [128, 8192], F32) as TPS:
                        for j in range(8):
                            P.step("dve", lambda h, TPS=TPS, buf=buf, j=j: h.tensor_copy(TPS[:, j * 1024:(j + 1) * 1024], buf[:, j * 1024:(j + 1) * 1024]), sem="dve")
                        P.want("sp", ("dve", P.cnt["dve"]))
                        P.step("sp", lambda h, TPS=TPS, tp_ap=tp_ap: h.dma_start(out=tp_ap, in_=TPS[:, :]), sem="cst", amt=16)
                        P.want("dve", ("cst", P.cnt["cst"]))
                P.full_barrier()
            if DEV_TAPS and G == 0 and deltanet:
                for tp_ap, buf in ((c.tapC, KG),):
                    with nc.sbuf_tensor(tag + "tpq", [128, 8192], F32) as TPS:
                        for j in range(8):
                            P.step("dve", lambda h, TPS=TPS, buf=buf, j=j: h.tensor_copy(TPS[:, j * 1024:(j + 1) * 1024], buf[:, j * 1024:(j + 1) * 1024]), sem="dve")
                        P.want("sp", ("dve", P.cnt["dve"]))
                        P.step("sp", lambda h, TPS=TPS, tp_ap=tp_ap: h.dma_start(out=tp_ap, in_=TPS[:, :]), sem="cst", amt=16)
                        P.want("dve", ("cst", P.cnt["cst"]))
                P.full_barrier()
            if True:
                P.want("sp", ("pe", P.cnt["pe"]), ("act", P.cnt["act"]), ("dve", P.cnt["dve"]))
                kp_ms = P.step("sp", lambda h: h.dma_start(out=KEEP[:, :], in_=dr["keep"]), sem="cst", amt=16)
                PSC = c.PSF[:, :]
                PCX = c.PSG[:, 0:512]
                PRS = c.PSG[:, 512:1024]
                PBC = c.PSG[:, 1024:1536]
                P.want("pe", *c.psum_guard)
                c.psum_guard = []
                kbuf = KBG if deltanet else KG
                qbuf = KG if deltanet else QG
                ctx_prev = None
                tail = {"fn": None}
                head_ctx = {}
                for hh in range(4):
                    qh = qbuf[:, hh * 2048:(hh + 1) * 2048]
                    for s in range(4):
                        exp_ms = []
                        P.want("act", ("pe", P.cnt["pe"]), ("dve", P.cnt["dve"]))
                        half_banks = [PSC[:, 0:1024], PSC[:, 1024:2048]]
                        exp_prev = {0: None, 1: None}
                        for bidx2 in range(2 * (s + 1)):
                            hb = bidx2 % 2
                            PH = half_banks[hb]
                            if exp_prev[hb] is not None:
                                P.want("pe", exp_prev[hb])
                            for j in range(2):
                                kt = bidx2 * 2 + j
                                def fsc(h, kt=kt, j=j, s=s, hh=hh, qh=qh, PH=PH):
                                    return h.matmul(PH[:, j * 512:(j + 1) * 512],
                                                    kbuf[:, hh * 2048 + kt * 128: hh * 2048 + (kt + 1) * 128],
                                                    qh[:, s * 512:(s + 1) * 512], start=True, stop=True)
                                pm = P.step("pe", fsc, sem="pe" if j == 1 else None)
                            P.want("act", pm)
                            am = P.step("act", lambda h, bidx2=bidx2, PH=PH: h.activation(
                                STEXP[:, bidx2 * 1024:(bidx2 + 1) * 1024], PH[:, :], AF.Exp,
                                scale=SCALE_ATT, bias=c.neg8[:, 0:1]), sem="act")
                            exp_ms.append(am)
                            exp_prev[hb] = am
                        P.want("dve", exp_ms[-1], kp_ms)
                        mk = None
                        for j0 in range(4):
                            kt = 4 * s + j0
                            mk = P.step("dve", lambda h, kt=kt, j0=j0: h.tensor_tensor(
                                STEXP[:, kt * 512:(kt + 1) * 512], STEXP[:, kt * 512:(kt + 1) * 512],
                                KEEP[:, j0 * 512:(j0 + 1) * 512], OP.mult), sem="dve")
                        if tail["fn"] is not None:
                            ctx_prev = tail["fn"]()
                            tail["fn"] = None
                        P.want("pe", mk, *exp_ms, *( [ctx_prev] if ctx_prev else [] ))
                        nkt = 4 * s + 4
                        for kt in range(nkt):
                            def frs(h, kt=kt, st=(kt == 0), sp=(kt == nkt - 1)):
                                return h.matmul(PRS[0:1, 0:512], c.ones_col_bf[:, :],
                                                STEXP[:, kt * 512:(kt + 1) * 512], start=st, stop=sp)
                            P.step("pe", frs, sem="pe" if kt == nkt - 1 else None)
                        for kt in range(nkt):
                            def fav(h, kt=kt, hh=hh, st=(kt == 0), sp=(kt == nkt - 1)):
                                return h.matmul(PCX[:, 0:512],
                                                VG[:, kt * 512 + hh * 128: kt * 512 + (hh + 1) * 128],
                                                STEXP[:, kt * 512:(kt + 1) * 512], start=st, stop=sp)
                            av = P.step("pe", fav, sem="pe" if kt == nkt - 1 else None)

                        def mk_tail(hh=hh, s=s, av=av, G=G):
                            def emit_tail():
                                P.want("act", av)
                                P.step("act", lambda h: h.activation(RECB[0:1, 0:512], PRS[0:1, 0:512], AF.Ln), sem="act")
                                rc = P.step("act", lambda h: h.activation(RECB[0:1, 0:512], RECB[0:1, 0:512], AF.Exp, scale=-1.0), sem="act")
                                P.want("pe", rc)
                                bc = P.step("pe", lambda h: h.matmul(PBC[:, 0:512], c.ones_row_bf[:, :], RECB[0:1, 0:512], start=True, stop=True), sem="pe")
                                P.want("act", bc)
                                ab = P.step("act", lambda h: h.activation(BCS[:, 0:512], PBC[:, :], AF.Copy), sem="act")
                                P.want("dve", ab, av)
                                cp = P.step("dve", lambda h, s=s: h.tensor_tensor(
                                    CTXF[:, s * 512:(s + 1) * 512], PCX[:, 0:512], BCS[:, 0:512], OP.mult), sem="dve")
                                if s == 3:
                                    gh = G * 4 + hh
                                    P.want("sp", cp, *getattr(c, "ctxst_free", []))
                                    cm = P.step("sp", lambda h, gh=gh: h.dma_start(
                                        out=c.ctx_st[gh * 128:(gh + 1) * 128, :], in_=CTXF[:, :]), sem="cs", amt=16)
                                    ctx_dma_ms.append(cm)
                                    P.want("dve", cm)
                                return cp
                            return emit_tail
                        tail["fn"] = mk_tail()
                if tail["fn"] is not None:
                    ctx_prev = tail["fn"]()
                    tail["fn"] = None
                c.psum_guard = [("dve", P.cnt["dve"]), ("act", P.cnt["act"]), ("pe", P.cnt["pe"])]
            P.want("po", ("cs", P.cnt["cs"]), *getattr(c, "ctxout_free", []))
            ccm = P.step("po", lambda h, G=G: h.collective_compute(
                "AllGather", OP.bypass, replica_groups=PAIRS,
                ins=[c.ctx_st[G * 512:(G + 1) * 512, :].opt()],
                outs=[c.ctx_out[G * 1024:(G + 1) * 1024, :].opt()]), sem="ccx", amt=1)
            c.ctx_cc_g[G] = ("ccx", P.cnt["ccx"])
            P.full_barrier()
        c.ctxst_free = [c.ctx_cc_g[1]]
        for g in range(4):
            c.xb_free[g] = [("pe", P.cnt["pe"])]
        c.xngo_free = {g: [xb_ms[g]] for g in range(4)}
        P.full_barrier()


def emit_scan(c, tag, G, KG, KBG, VG):
    import contextlib
    P, nc = c.P, c.nc
    with contextlib.ExitStack() as es:
        sb = lambda n, s, d: es.enter_context(nc.sbuf_tensor(tag + f"g{G}s" + n, list(s), d))
        TJ2 = sb("TJ2", (128, 2 * 7 * 512), BF16)
        TJp = [TJ2[:, 0:7 * 512], TJ2[:, 7 * 512:14 * 512]]
        PJ = sb("PJ", (128, 512), BF16)
        SSB = sb("SSB", (128, 512), BF16)
        SSB32 = sb("SSB32", (128, 512), F32)
        TMPB = sb("TMPB", (128, 512), BF16)
        KBT2 = sb("KBT2", (128, 1024), BF16)
        KBTp = [KBT2[:, 0:512], KBT2[:, 512:1024]]
        PA = c.PSF[:, 0:512]
        PBt = c.PSF[:, 512:1024]
        PS4 = c.PSF[:, 1024:1536]
        TMA = c.PSF[:, 1536:2048]
        PSQ = c.PSG[:, 0:512]
        PTR = c.PSG[:, 512:1024].bitcast(BF16)
        TMB = c.PSG[:, 1536:2048]
        P.want("pe", *c.psum_guard)
        c.psum_guard = []
        KG4 = KG[:, :].rearrange("p (hh t) -> p hh t", hh=4)
        st = {"s_ev": None, "last_apply": None, "last_tma": None, "vn_ev": None, "pw_dve": None}

        def powers_abuild(ch):
            p = ch % 2
            TJ = TJp[p]
            if st["pw_dve"]:
                P.want("pe", st["pw_dve"])
            for hh in range(4):
                P.step("pe", lambda h, hh=hh, ch=ch: h.matmul(
                    PA[:, hh * 128:(hh + 1) * 128],
                    KG[:, hh * 2048 + ch * 128: hh * 2048 + (ch + 1) * 128],
                    KBG[:, hh * 2048 + ch * 128: hh * 2048 + (ch + 1) * 128],
                    start=True, stop=True))
            for hh in range(4):
                pma = P.step("pe", lambda h, hh=hh, ch=ch: h.matmul(
                    PBt[:, hh * 128:(hh + 1) * 128],
                    KBG[:, hh * 2048 + ch * 128: hh * 2048 + (ch + 1) * 128],
                    KG[:, hh * 2048 + ch * 128: hh * 2048 + (ch + 1) * 128],
                    start=True, stop=True), sem="pe" if hh == 3 else None)
            for hh in range(4):
                ptr = P.step("pe", lambda h, hh=hh, ch=ch: h.transpose(
                    PTR[:, 512 + hh * 128: 512 + (hh + 1) * 128],
                    KBG[:, hh * 2048 + ch * 128: hh * 2048 + (ch + 1) * 128],
                    c.ident[:, :]), sem="pe" if hh == 3 else None)
            P.want("dve", pma, ptr)
            dj = P.step("dve", lambda h: h.tensor_tensor(PJ[:, :], PA[:, 0:512], c.maskl4[:, :], OP.mult), sem="dve")
            P.step("dve", lambda h, TJ=TJ: h.tensor_tensor(TJ[:, 0:512], PBt[:, 0:512], c.masku4[:, :], OP.mult), sem="dve")
            dng = P.step("dve", lambda h, p=p: h.tensor_copy(KBTp[p][:, :], PTR[:, 512:1024]), sem="dve")
            st["pw_dve"] = dng
            return dng

        def powers_level(ch, j):
            p = ch % 2
            TJ = TJp[p]
            P.want("pe", st["pw_dve"])
            for hh in range(4):
                psq = P.step("pe", lambda h, hh=hh, j=j, TJ=TJ: h.matmul(
                    PSQ[:, hh * 128:(hh + 1) * 128],
                    TJ[:, j * 512 + hh * 128: j * 512 + (hh + 1) * 128],
                    PJ[:, hh * 128:(hh + 1) * 128], start=True, stop=True),
                    sem="pe" if hh == 3 else None)
            P.want("dve", psq)
            dpj = P.step("dve", lambda h: h.tensor_copy(PJ[:, :], PSQ[:, 0:512]), sem="dve")
            P.want("pe", dpj)
            for hh in range(4):
                ptr = P.step("pe", lambda h, hh=hh: h.transpose(
                    PTR[:, hh * 128:(hh + 1) * 128], PJ[:, hh * 128:(hh + 1) * 128],
                    c.ident[:, :]), sem="pe" if hh == 3 else None)
            P.want("dve", ptr)
            st["pw_dve"] = P.step("dve", lambda h, j=j, TJ=TJ: h.tensor_copy(TJ[:, (j + 1) * 512:(j + 2) * 512], PTR[:, 0:512]), sem="dve")

        def applies_rhs(ch):
            P.want("pe", *( [st["s_ev"]] if st["s_ev"] else [] ),
                   *( [st["last_apply"]] if st["last_apply"] else [] ))
            lm = None
            for hh in range(4):
                lm = P.step("pe", lambda h, hh=hh, ch=ch: h.matmul(
                    TMA[:, hh * 128:(hh + 1) * 128], c.ident[:, :],
                    VG[:, ch * 512 + hh * 128: ch * 512 + (hh + 1) * 128],
                    start=(hh == 0), stop=(ch == 0 and hh == 3)),
                    sem="pe" if (ch == 0 and hh == 3) else None)
            if ch > 0:
                for hh in range(4):
                    lm = P.step("pe", lambda h, hh=hh, ch=ch: h.matmul(
                        TMA[:, hh * 128:(hh + 1) * 128],
                        KG[:, hh * 2048 + ch * 128: hh * 2048 + (ch + 1) * 128],
                        SSB[:, hh * 128:(hh + 1) * 128], start=False, stop=(hh == 3)),
                        sem="pe" if hh == 3 else None)
            st["last_apply"] = lm

        def apply_level(ch, j, cur, nxt):
            p = ch % 2
            TJ = TJp[p]
            P.want("act", st["last_apply"] if st["last_apply"] else ("pe", P.cnt["pe"]))
            atm = P.step("act", lambda h, cur=cur: h.activation(TMPB[:, :], cur, AF.Copy), sem="act")
            P.want("pe", atm)
            for hh in range(4):
                P.step("pe", lambda h, hh=hh, nxt=nxt: h.matmul(
                    nxt[:, hh * 128:(hh + 1) * 128], c.ident[:, :],
                    TMPB[:, hh * 128:(hh + 1) * 128], start=(hh == 0), stop=False))
            for hh in range(4):
                lm = P.step("pe", lambda h, hh=hh, j=j, nxt=nxt, TJ=TJ: h.matmul(
                    nxt[:, hh * 128:(hh + 1) * 128],
                    TJ[:, j * 512 + hh * 128: j * 512 + (hh + 1) * 128],
                    TMPB[:, hh * 128:(hh + 1) * 128], start=False, stop=(hh == 3)),
                    sem="pe" if hh == 3 else None)
            st["last_apply"] = lm

        def vn_s(ch, cur):
            p = ch % 2
            P.want("dve", st["last_apply"])
            vn_ev = P.step("dve", lambda h, ch=ch, cur=cur: h.tensor_copy(VG[:, ch * 512:(ch + 1) * 512], cur), sem="dve")
            P.want("pe", vn_ev)
            for hh in range(4):
                psu = P.step("pe", lambda h, hh=hh, ch=ch, p=p: h.matmul(
                    PS4[:, hh * 128:(hh + 1) * 128], KBTp[p][:, hh * 128:(hh + 1) * 128],
                    VG[:, ch * 512 + hh * 128: ch * 512 + (hh + 1) * 128],
                    start=(hh == 0), stop=(hh == 3)), sem="pe" if hh == 3 else None)
            if ch < 15:
                P.want("dve", psu)
                if ch == 0:
                    P.step("dve", lambda h: h.tensor_copy(SSB32[:, :], PS4[:, 0:512]), sem="dve")
                else:
                    P.step("dve", lambda h: h.tensor_tensor(SSB32[:, :], PS4[:, 0:512], SSB32[:, :], OP.add), sem="dve")
                st["s_ev"] = P.step("dve", lambda h: h.tensor_scalar(SSB[:, :], SSB32[:, :], -1.0, None, OP.mult, OP.bypass), sem="dve")

        powers_abuild(0)
        for j in range(6):
            powers_level(0, j)
        for ch in range(16):
            applies_rhs(ch)
            if ch + 1 < 16:
                powers_abuild(ch + 1)
            cur, nxt = TMA, TMB
            for j in range(7):
                apply_level(ch, j, cur, nxt)
                if ch + 1 < 16 and j < 6:
                    powers_level(ch + 1, j)
                cur, nxt = nxt, cur
            vn_s(ch, cur)
        c.psum_guard = [("dve", P.cnt["dve"]), ("act", P.cnt["act"]), ("pe", P.cnt["pe"])]


def emit_attn_post(c, kind):
    import contextlib
    P, nc, dr = c.P, c.nc, c.dr
    pre = "b" if kind == "attn_b" else "a"
    tag = f"o{pre}_"
    P.full_barrier()
    with contextlib.ExitStack() as es:
        sb = lambda n, s, d: es.enter_context(nc.sbuf_tensor(tag + n, list(s), d))
        CTA = sb("CTA", (128, 16 * 1024), BF16)
        CTB = [sb(f"CTB{i}", (128, 1024), BF16) for i in range(2)]
        PD = [c.PSF[:, 0:512], c.PSF[:, 512:1024], c.PSF[:, 1024:1536], c.PSF[:, 1536:2048]]
        P.want("pe", *c.psum_guard)
        c.psum_guard = []
        sel_ms = []
        sel_d = {}
        order = [0, 1, 2, 3, 8, 9, 10, 11, 4, 5, 6, 7, 12, 13, 14, 15]
        for idx, cidx in enumerate(order):
            gg = (cidx % 8) // 4
            src_row = gg * 1024 + (cidx // 8) * 512 + (cidx % 4) * 128
            P.want("sp", c.ctx_cc_g[gg], *(sel_ms[-2:-1] if idx >= 2 else []))
            P.step("sp", lambda h, cidx=cidx, src_row=src_row: h.dma_start(
                out=CTA[:, cidx * 1024:(cidx + 1) * 1024],
                in_=c.ctx_out[src_row:src_row + 128, 0:1024]), sem=f"wo{idx % 2}", amt=16)
            P.step("sp", lambda h, cidx=cidx, src_row=src_row, idx=idx: h.dma_start(
                out=CTB[idx % 2][:, :],
                in_=c.ctx_out[src_row:src_row + 128, 1024:2048]), sem=f"wo{idx % 2}", amt=16)
            P.want("dve", (f"wo{idx % 2}", P.cnt[f"wo{idx % 2}"]))
            P.step("dve", lambda h, cidx=cidx: h.tensor_scalar(
                CTA[:, cidx * 1024:(cidx + 1) * 1024], CTA[:, cidx * 1024:(cidx + 1) * 1024],
                c.npar[:, 0:1], None, OP.mult, OP.bypass), sem="dve")
            dm = P.step("dve", lambda h, cidx=cidx, idx=idx: h.scalar_tensor_tensor(
                CTA[:, cidx * 1024:(cidx + 1) * 1024], CTB[idx % 2][:, :], c.par[:, 0:1],
                CTA[:, cidx * 1024:(cidx + 1) * 1024], OP.mult, OP.add), sem="dve")
            sel_ms.append(dm)
            sel_d[cidx] = dm
        def stage_wo(d, buf):
            P.want("po", *c.stage_guard.get(f"wstB{buf}") or [])
            return P.step("po", lambda h, d=d, buf=buf: h.dma_start(
                out=c.wstB[buf][:, :].rearrange("p (q dd) -> p q dd", q=KC),
                in_=dr[pre + "Wo"][:, d * 128:(d + 1) * 128].rearrange("(q p) dd -> p q dd", p=128)), sem=f"wB{buf}", amt=16)
        wo_ms = {0: stage_wo(0, 0)}
        pd_free = {0: [], 1: [], 2: [], 3: []}
        for d in range(16):
            if d + 1 < 16:
                wo_ms[d + 1] = stage_wo(d + 1, (d + 1) % 2)
            for half in range(2):
                slot = (2 * d + half) % 4
                Pd = PD[slot]
                P.want("pe", wo_ms[d], *pd_free[slot])
                for ci_i, cidx in enumerate(order):
                    P.want("pe", sel_d[cidx])
                    def fwo(h, cidx=cidx, d=d, half=half, Pd=Pd, st=(ci_i == 0), sp=(ci_i == 15)):
                        return h.matmul(Pd[:, 0:512], c.wstB[d % 2][:, cidx * 128:(cidx + 1) * 128],
                                        CTA[:, cidx * 1024 + half * 512: cidx * 1024 + half * 512 + 512],
                                        start=st, stop=sp)
                    pm = P.step("pe", fwo, sem="pe" if ci_i == 15 else None)
                P.want("dve", pm, *(c.xr_chunk[d] if half == 0 else []))
                xr_sl = c.XR[:, d * TOK + half * 512: d * TOK + half * 512 + 512]
                em = P.step("dve", lambda h, Pd=Pd, xr_sl=xr_sl: h.tensor_tensor(xr_sl, Pd[:, 0:512], xr_sl, OP.add), sem="dve")
                pd_free[slot] = [em]
                if half == 1:
                    c.xr_chunk[d] = [em]
            c.stage_guard[f"wstB{d % 2}"] = [("pe", P.cnt["pe"])]
        c.ctxout_free = [("pe", P.cnt["pe"])]
        c.xr_ready = [("dve", P.cnt["dve"])]
        c.xr_dirty = True
        c.psum_guard = [("dve", P.cnt["dve"])]
        P.full_barrier()


def kernel(**inputs):
    in_maps = host_prep(inputs)
    nc, c = build_program()
    res = run_bass_kernel_spmd(nc, in_maps, core_ids=list(range(8)))
    out = np.empty((B, S, D), np.float32)
    for core in range(8):
        b, half = core // 2, core % 2
        out[b, half * TOK:(half + 1) * TOK, :] = res.results[core]["out"].T
    return out
